# version 12
# baseline (speedup 1.0000x reference)
import numpy as np
from contextlib import ExitStack

import concourse.bass as bass
import concourse.mybir as mybir
from concourse.bass_utils import run_bass_kernel_spmd

F32 = mybir.dt.float32
BF16 = mybir.dt.bfloat16
I32 = mybir.dt.int32
AF = mybir.ActivationFunctionType
OP = mybir.AluOpType
AXX = mybir.AxisListType.X

B = 2
T = 4096
D = 4096
DC = 32
NH = 32
I = 11008
ICH = 86
QW = 512
NKC = 32
TTW = 256
NTT = T // TTW
NPASS = 3
OCSPL = [0, 29, 58, ICH]
ALPHA = 1.0 / np.sqrt(128.0)
Q_OFF = -7.5
Q_STEP = 0.005
LR_SCALE = 0.02
NEG = -30000.0


class Sem:
    def __init__(self, h):
        self.h = h
        self.n = 0


def build_program(debug=False):
    nc = bass.Bass("TRN2", target_bir_lowering=False)

    xkv = nc.dram_tensor("xkv", [B, NTT, 128, DC, TTW], F32, kind="ExternalInput").ap()
    xown_d = nc.dram_tensor("xown", [B, DC, 128, QW], F32, kind="ExternalInput").ap()
    wk_d = nc.dram_tensor("wk", [D, 128], F32, kind="ExternalInput").ap()
    wv_d = nc.dram_tensor("wv", [D, 128], F32, kind="ExternalInput").ap()
    wq_d = nc.dram_tensor("wq_blk", [DC, 128, DC, 128], F32, kind="ExternalInput").ap()
    wo_d = nc.dram_tensor("wo_blk", [DC, 128, DC, 128], F32, kind="ExternalInput").ap()
    g_d = nc.dram_tensor("gate_blk", [ICH, 128, DC, 128], I32, kind="ExternalInput").ap()
    u_d = nc.dram_tensor("up_blk", [ICH, 128, DC, 128], I32, kind="ExternalInput").ap()
    dn_d = nc.dram_tensor("down_blk", [DC, 128, ICH, 128], I32, kind="ExternalInput").ap()
    gsc_d = nc.dram_tensor("gsc", [128, ICH], F32, kind="ExternalInput").ap()
    usc_d = nc.dram_tensor("usc", [128, ICH], F32, kind="ExternalInput").ap()
    dsc_d = nc.dram_tensor("dsc", [128, DC], F32, kind="ExternalInput").ap()
    gA_d = nc.dram_tensor("gA", [128, DC, 32], F32, kind="ExternalInput").ap()
    gB_d = nc.dram_tensor("gB", [32, I], F32, kind="ExternalInput").ap()
    uA_d = nc.dram_tensor("uA", [128, DC, 32], F32, kind="ExternalInput").ap()
    uB_d = nc.dram_tensor("uB", [32, I], F32, kind="ExternalInput").ap()
    dA_d = nc.dram_tensor("dA", [128, ICH, 32], F32, kind="ExternalInput").ap()
    dB_d = nc.dram_tensor("dB", [32, D], F32, kind="ExternalInput").ap()
    vones_d = nc.dram_tensor("vones", [B, NKC], F32, kind="ExternalInput").ap()

    attnC = nc.dram_tensor("attnC", [B, D, QW], F32, kind="Internal").ap()
    partial = nc.dram_tensor("partial", [DC, 128, B * QW], F32, kind="Internal").ap()

    youts = [nc.dram_tensor(f"y{b}", [D, QW], F32, kind="ExternalOutput").ap()
             for b in range(B)]

    jj, pp, qq = np.meshgrid(np.arange(4), np.arange(128), np.arange(QW), indexing="ij")
    import ml_dtypes as _mld
    tri_np = np.where(jj * 128 + pp <= qq, 0.0, NEG).astype(_mld.bfloat16)
    tri_d = nc.inline_tensor(tri_np, name="trimask").ap()
    import ml_dtypes
    ident_d = nc.inline_tensor(np.eye(128, dtype=ml_dtypes.bfloat16), name="ident").ap()

    dbg = {}
    if debug:
        dbg["xnown"] = nc.dram_tensor("dbg_xnown", [128, DC, B * QW], BF16, kind="ExternalOutput").ap()
        dbg["K"] = nc.dram_tensor("dbg_K", [128, B, T], BF16, kind="ExternalOutput").ap()
        dbg["Vt"] = nc.dram_tensor("dbg_Vt", [128, B, NKC, 129], BF16, kind="ExternalOutput").ap()
        dbg["QT"] = nc.dram_tensor("dbg_QT", [B, 128, NH, QW], BF16, kind="ExternalOutput").ap()
        dbg["attnT"] = nc.dram_tensor("dbg_attnT", [B, 128, NH, QW], BF16, kind="ExternalOutput").ap()
        dbg["negm"] = nc.dram_tensor("dbg_negm", [B, 128, 1], F32, kind="ExternalOutput").ap()
        dbg["small"] = nc.dram_tensor("dbg_small", [B, 16], F32, kind="ExternalOutput").ap()
        dbg["smallbb"] = nc.dram_tensor("dbg_smallbb", [B, 16], BF16, kind="ExternalOutput").ap()
        dbg["kq"] = nc.dram_tensor("dbg_kq", [B, 2 * B], F32, kind="ExternalOutput").ap()
        dbg["qnrow"] = nc.dram_tensor("dbg_qnrow", [B, NH], F32, kind="ExternalOutput").ap()
        dbg["hbuf"] = nc.dram_tensor("dbg_hbuf", [128, 28, B * QW], BF16, kind="ExternalOutput").ap()
        dbg["xA"] = nc.dram_tensor("dbg_xA", [32, 4, QW], BF16, kind="ExternalOutput").ap()
        dbg["hd"] = nc.dram_tensor("dbg_hd", [32, B * QW], F32, kind="ExternalOutput").ap()
        dbg["attnC"] = nc.dram_tensor("dbg_attnC", [B, D, QW], F32, kind="ExternalOutput").ap()

    with ExitStack() as ctx:
        E = ctx.enter_context
        xnown = E(nc.sbuf_tensor("xnown", [128, DC, B * QW], BF16))
        ones128 = E(nc.sbuf_tensor("ones128", [128, 1], BF16))
        ones1r = E(nc.sbuf_tensor("ones1r", [1, 128], BF16))
        vonesrow = E(nc.sbuf_tensor("vonesrow", [1, B * NKC], BF16))
        sgates = E(nc.sbuf_tensor("sgates", [128, ICH], F32))
        sups = E(nc.sbuf_tensor("sups", [128, ICH], F32))
        sdowns = E(nc.sbuf_tensor("sdowns", [128, DC], F32))
        kmx = E(nc.sbuf_tensor("kmx", [1, B], F32))
        qmx = E(nc.sbuf_tensor("qmx", [1, B], F32))
        smallf = E(nc.sbuf_tensor("smallf", [1, 16], F32))
        smallb = E(nc.sbuf_tensor("smallb", [1, 16], BF16))
        negm128 = E(nc.sbuf_tensor("negm128", [128, 1], F32))
        vdum = E(nc.sbuf_tensor("vdum", [1, 8], F32))

        PS1 = E(nc.psum_tensor("PS1", [128, 4, 512], F32))
        PS2 = E(nc.psum_tensor("PS2", [128, 2, 512], F32))
        PTRs = [E(nc.psum_tensor("PTRa", [128, 128], BF16)),
                E(nc.psum_tensor("PTRb", [128, 128], BF16))]

        names = ["ldw", "ldx", "ldc", "v", "s", "t", "sac", "spp", "pld", "sto"]
        S = {n: Sem(E(nc.semaphore(n))) for n in names}

        attn_scope = ExitStack()
        Ksb = attn_scope.enter_context(nc.sbuf_tensor("Ksb", [128, B, T], BF16))
        Vt = attn_scope.enter_context(nc.sbuf_tensor("Vt", [128, B, NKC, 129], BF16))
        masksb = attn_scope.enter_context(nc.sbuf_tensor("masksb", [128, 4, QW], BF16))
        identsb = attn_scope.enter_context(nc.sbuf_tensor("identsb", [128, 128], BF16))

        with nc.Block() as blk:
            @blk.sync
            def _(sync):
                sync.dma_start(out=masksb[:], in_=tri_d.rearrange("j p q -> p j q")).then_inc(S["ldc"].h, 16)
                sync.dma_start(out=identsb[:], in_=ident_d[:, :]).then_inc(S["ldc"].h, 16)
                sync.dma_start(out=sgates[:], in_=gsc_d[:, :]).then_inc(S["ldc"].h, 16)
                sync.dma_start(out=sups[:], in_=usc_d[:, :]).then_inc(S["ldc"].h, 16)
                sync.dma_start(out=sdowns[:], in_=dsc_d[:, :]).then_inc(S["ldc"].h, 16)
                S["ldc"].n += 5 * 16

            v0 = S["v"].n + 1
            t0 = S["t"].n + 1

            @blk.gpsimd
            def _(g):
                g.dma_start(out=vonesrow[0:1, :], in_=vones_d[None, :, :].rearrange("o b k -> o (b k)")).then_inc(S["ldc"].h, 16)
                S["ldc"].n += 16

            @blk.vector
            def _(v):
                v.wait_ge(S["ldc"].h, S["ldc"].n)
                v.memset(ones128[:], 1.0)
                v.memset(ones1r[:], 1.0)
                v.tensor_scalar_mul(sgates[:], sgates[:], Q_STEP)
                v.tensor_scalar_mul(sups[:], sups[:], Q_STEP)
                v.tensor_scalar_mul(sdowns[:], sdowns[:], Q_STEP).then_inc(S["v"].h, 1)
                S["v"].n += 1
                v.wait_ge(S["t"].h, t0)
                v.tensor_copy(Vt[:, 0, :, 128], PS2[:, 0, 0:NKC])
                v.tensor_copy(Vt[:, 1, :, 128], PS2[:, 0, NKC:2 * NKC]).then_inc(S["v"].h, 1)
                S["v"].n += 1

            @blk.tensor
            def _(t):
                t.wait_ge(S["v"].h, v0)
                t.matmul(PS2[:, 0, 0:B * NKC], ones1r[:], vonesrow[:],
                         start=True, stop=True).then_inc(S["t"].h, 1)
                S["t"].n += 1

        with ExitStack() as p1:
            E1 = p1.enter_context
            wkb = E1(nc.sbuf_tensor("wkb", [128, DC, 128], BF16))
            wvb = E1(nc.sbuf_tensor("wvb", [128, DC, 128], BF16))
            xbb = E1(nc.sbuf_tensor("xbb", [128, 2, DC, TTW], BF16))
            xsqb = E1(nc.sbuf_tensor("xsqb", [128, 2, DC, TTW], BF16))
            rowsf = E1(nc.sbuf_tensor("rowsf", [1, 4, TTW], F32))
            rowsb = E1(nc.sbuf_tensor("rowsb", [1, 2, TTW], BF16))
            bcb = E1(nc.sbuf_tensor("bcb", [128, 2, 2, TTW], BF16))
            ksq = E1(nc.sbuf_tensor("ksq", [128, TTW], BF16))
            knrow = E1(nc.sbuf_tensor("knrow", [1, NTT], F32))

            with nc.Block() as blk:
                @blk.gpsimd
                def _(g):
                    g.dma_start(out=wkb[:], in_=wk_d.rearrange("(c p) h -> p c h", p=128)).then_inc(S["ldc"].h, 16)
                    g.dma_start(out=wvb[:], in_=wv_d.rearrange("(c p) h -> p c h", p=128)).then_inc(S["ldc"].h, 16)
                    S["ldc"].n += 32

                @blk.vector
                def _(v):
                    v.wait_ge(S["ldc"].h, S["ldc"].n)

            tiles = [(b, tt) for b in range(B) for tt in range(NTT)]
            NTL = len(tiles)

            vb, sb_, tb = S["v"].n, S["s"].n, S["t"].n
            ldxb = S["ldx"].n
            v_sread, v_rows, v_norm, v_kvev, v_kn = {}, {}, {}, {}, {}
            v_bcb = {}
            s_xsq, s_rstd, s_ksq = {}, {}, {}
            t_stats, t_bc, t_kv, t_kn = {}, {}, {}, {}
            vc, sc_, tc = vb, sb_, tb
            for i, (b, tt) in enumerate(tiles):
                sc_ += 1; s_xsq[i] = sc_
                tc += 1; t_stats[i] = tc
                if i >= 1:
                    tc += 1; t_kv[i - 1] = tc
                vc += 1; v_sread[i] = vc
                sc_ += 1; s_rstd[i] = sc_
                if i >= 1:
                    sc_ += 1; s_ksq[i - 1] = sc_
                vc += 1; v_rows[i] = vc
                tc += 1; t_bc[i] = tc
                if i >= 1:
                    tc += 1; t_kn[i - 1] = tc
                vc += 1; v_bcb[i] = vc
                if i >= 1:
                    vc += 1; v_kvev[i - 1] = vc
                vc += 1; v_norm[i] = vc
                if i >= 1:
                    vc += 1; v_kn[i - 1] = vc
                    if tiles[i - 1][1] == NTT - 1:
                        vc += 1
            tc += 1; t_kv[NTL - 1] = tc
            tc += 1; t_kn[NTL - 1] = tc
            sc_ += 1; s_ksq[NTL - 1] = sc_
            vc += 1; v_kvev[NTL - 1] = vc
            vc += 1; v_kn[NTL - 1] = vc
            vc += 1

            with nc.Block() as blk:
                @blk.gpsimd
                def _(g):
                    for i, (b, tt) in enumerate(tiles):
                        if i >= 2:
                            g.wait_ge(S["t"].h, t_kv[i - 2])
                        g.dma_start(out=xbb[:, i % 2], in_=xkv[b, tt]).then_inc(S["ldx"].h, 16)
                        S["ldx"].n += 16

                @blk.scalar
                def _(s):
                    for i, (b, tt) in enumerate(tiles):
                        s.wait_ge(S["ldx"].h, ldxb + 16 * (i + 1))
                        if i >= 2:
                            s.wait_ge(S["t"].h, t_stats[i - 2])
                        s.activation(xsqb[:, i % 2].rearrange("p c t -> p (c t)"),
                                     xbb[:, i % 2].rearrange("p c t -> p (c t)"),
                                     AF.Square).then_inc(S["s"].h, 1)
                        S["s"].n += 1
                        s.wait_ge(S["v"].h, v_sread[i])
                        s.activation(rowsf[0:1, 3, :], rowsf[0:1, 2, :], AF.Sqrt).then_inc(S["s"].h, 1)
                        S["s"].n += 1
                        if i >= 1:
                            b2, tt2 = tiles[i - 1]
                            s.wait_ge(S["v"].h, v_kvev[i - 1])
                            s.activation(ksq[:], Ksb[:, b2, tt2 * TTW:(tt2 + 1) * TTW],
                                         AF.Square).then_inc(S["s"].h, 1)
                            S["s"].n += 1
                    b2, tt2 = tiles[NTL - 1]
                    s.wait_ge(S["v"].h, v_kvev[NTL - 1])
                    s.activation(ksq[:], Ksb[:, b2, tt2 * TTW:(tt2 + 1) * TTW],
                                 AF.Square).then_inc(S["s"].h, 1)
                    S["s"].n += 1

                @blk.tensor
                def _(t):
                    def kv_group(i2):
                        t.wait_ge(S["v"].h, v_norm[i2])
                        if i2 >= 1:
                            t.wait_ge(S["v"].h, v_kvev[i2 - 1])
                        for c in range(DC):
                            t.matmul(PS1[:, 0, 0:TTW], wkb[:, c, :], xbb[:, i2 % 2, c, :],
                                     start=(c == 0), stop=(c == DC - 1))
                        for sub in range(2):
                            for c in range(DC):
                                m = t.matmul(PS1[:, 1, sub * 128:(sub + 1) * 128],
                                             xbb[:, i2 % 2, c, sub * 128:(sub + 1) * 128],
                                             wvb[:, c, :],
                                             start=(c == 0), stop=(c == DC - 1))
                        m.then_inc(S["t"].h, 1); S["t"].n += 1

                    def kn_group(i2):
                        t.wait_ge(S["s"].h, s_ksq[i2])
                        if i2 >= 1:
                            t.wait_ge(S["v"].h, v_kn[i2 - 1])
                        t.matmul(PS1[0:1, 2, 0:TTW], ones128[:], ksq[:],
                                 start=True, stop=True).then_inc(S["t"].h, 1)
                        S["t"].n += 1

                    for i, (b, tt) in enumerate(tiles):
                        t.wait_ge(S["s"].h, s_xsq[i])
                        if i >= 1:
                            t.wait_ge(S["v"].h, v_bcb[i - 1])
                        for c in range(DC):
                            t.matmul(PS2[0:1, 0, 0:TTW], ones128[:], xbb[:, i % 2, c, :],
                                     start=(c == 0), stop=(c == DC - 1))
                            m = t.matmul(PS2[0:1, 1, 0:TTW], ones128[:], xsqb[:, i % 2, c, :],
                                         start=(c == 0), stop=(c == DC - 1))
                        m.then_inc(S["t"].h, 1); S["t"].n += 1
                        if i >= 1:
                            kv_group(i - 1)
                        t.wait_ge(S["v"].h, v_rows[i])
                        t.matmul(PS2[:, 0, 0:TTW], ones1r[:], rowsb[0:1, 0, :], start=True, stop=True)
                        t.matmul(PS2[:, 1, 0:TTW], ones1r[:], rowsb[0:1, 1, :],
                                 start=True, stop=True).then_inc(S["t"].h, 1)
                        S["t"].n += 1
                        if i >= 1:
                            kn_group(i - 1)
                    kv_group(NTL - 1)
                    kn_group(NTL - 1)

                @blk.vector
                def _(v):
                    for i, (b, tt) in enumerate(tiles):
                        v.wait_ge(S["t"].h, t_stats[i])
                        v.tensor_scalar_mul(rowsf[0:1, 0, :], PS2[0:1, 0, 0:TTW], 1.0 / D)
                        v.tensor_scalar_mul(rowsf[0:1, 1, :], PS2[0:1, 1, 0:TTW], 1.0 / D)
                        v.tensor_mul(rowsf[0:1, 2, :], rowsf[0:1, 0, :], rowsf[0:1, 0, :])
                        v.tensor_sub(rowsf[0:1, 1, :], rowsf[0:1, 1, :], rowsf[0:1, 2, :])
                        v.tensor_scalar_add(rowsf[0:1, 1, :], rowsf[0:1, 1, :], 1e-5)
                        v.reciprocal(rowsf[0:1, 2, :], rowsf[0:1, 1, :]).then_inc(S["v"].h, 1)
                        S["v"].n += 1
                        v.wait_ge(S["s"].h, s_rstd[i])
                        v.tensor_mul(rowsf[0:1, 0, :], rowsf[0:1, 0, :], rowsf[0:1, 3, :])
                        v.tensor_scalar_mul(rowsb[0:1, 1, :], rowsf[0:1, 0, :], -1.0)
                        v.tensor_scalar_mul(rowsb[0:1, 0, :], rowsf[0:1, 3, :], 1.0).then_inc(S["v"].h, 1)
                        S["v"].n += 1
                        v.wait_ge(S["t"].h, t_bc[i])
                        v.tensor_copy(bcb[:, i % 2, 0, :], PS2[:, 0, 0:TTW])
                        v.tensor_copy(bcb[:, i % 2, 1, :], PS2[:, 1, 0:TTW]).then_inc(S["v"].h, 1)
                        S["v"].n += 1
                        if i >= 1:
                            b2, tt2 = tiles[i - 1]
                            v.wait_ge(S["t"].h, t_kv[i - 1])
                            v.tensor_copy(Ksb[:, b2, tt2 * TTW:(tt2 + 1) * TTW], PS1[:, 0, 0:TTW])
                            v.tensor_copy(Vt[:, b2, 2 * tt2, 0:128], PS1[:, 1, 0:128])
                            v.tensor_copy(Vt[:, b2, 2 * tt2 + 1, 0:128],
                                          PS1[:, 1, 128:256]).then_inc(S["v"].h, 1)
                            S["v"].n += 1
                        xv = xbb[:, i % 2]
                        v.tensor_mul(xv, xv, bcb[:, i % 2, 0:1, :].broadcast_to([128, DC, TTW]))
                        if tt * TTW < QW:
                            v.tensor_add(xv, xv, bcb[:, i % 2, 1:2, :].broadcast_to([128, DC, TTW]))
                            v.tensor_copy(
                                xnown[:, :, b * QW + tt * TTW: b * QW + (tt + 1) * TTW],
                                xv).then_inc(S["v"].h, 1)
                        else:
                            v.tensor_add(xv, xv,
                                         bcb[:, i % 2, 1:2, :].broadcast_to([128, DC, TTW])).then_inc(S["v"].h, 1)
                        S["v"].n += 1
                        if i >= 1:
                            b2, tt2 = tiles[i - 1]
                            v.wait_ge(S["t"].h, t_kn[i - 1])
                            v.reduce_max(knrow[0:1, tt2:tt2 + 1], PS1[0:1, 2, 0:TTW],
                                         axis=AXX).then_inc(S["v"].h, 1)
                            S["v"].n += 1
                            if tt2 == NTT - 1:
                                v.memset(vdum[:], 0.0)
                                v.reduce_max(kmx[0:1, b2:b2 + 1], knrow[0:1, :],
                                             axis=AXX)
                                v.memset(vdum[:], 0.0).then_inc(S["v"].h, 1)
                                S["v"].n += 1
                    b2, tt2 = tiles[NTL - 1]
                    v.wait_ge(S["t"].h, t_kv[NTL - 1])
                    v.tensor_copy(Ksb[:, b2, tt2 * TTW:(tt2 + 1) * TTW], PS1[:, 0, 0:TTW])
                    v.tensor_copy(Vt[:, b2, 2 * tt2, 0:128], PS1[:, 1, 0:128])
                    v.tensor_copy(Vt[:, b2, 2 * tt2 + 1, 0:128],
                                  PS1[:, 1, 128:256]).then_inc(S["v"].h, 1)
                    S["v"].n += 1
                    v.wait_ge(S["t"].h, t_kn[NTL - 1])
                    v.reduce_max(knrow[0:1, tt2:tt2 + 1], PS1[0:1, 2, 0:TTW],
                                 axis=AXX).then_inc(S["v"].h, 1)
                    S["v"].n += 1
                    v.memset(vdum[:], 0.0)
                    v.reduce_max(kmx[0:1, b2:b2 + 1], knrow[0:1, :], axis=AXX)
                    v.memset(vdum[:], 0.0).then_inc(S["v"].h, 1)
                    S["v"].n += 1

            assert S["v"].n == vc and S["s"].n == sc_ and S["t"].n == tc, \
                (S["v"].n, vc, S["s"].n, sc_, S["t"].n, tc)

        for b in range(B):
            with ExitStack() as pb:
                Eb = pb.enter_context
                QT = Eb(nc.sbuf_tensor(f"QT{b}", [128, NH, QW], BF16))
                attnT = Eb(nc.sbuf_tensor(f"attnT{b}", [128, NH, QW], BF16))
                wbf = Eb(nc.sbuf_tensor(f"wbf{b}", [128, 2, DC, 128], BF16))
                qsqb = Eb(nc.sbuf_tensor(f"qsqb{b}", [128, 2, QW], BF16))
                qnrow = Eb(nc.sbuf_tensor(f"qnrow{b}", [1, NH], F32))
                PT = Eb(nc.sbuf_tensor(f"PT{b}", [128, 4, QW], BF16))
                attok = Eb(nc.sbuf_tensor(f"attok{b}", [128, 4, 128], BF16))
                recb = Eb(nc.sbuf_tensor(f"recb{b}", [128, 4], F32))
                xres = Eb(nc.sbuf_tensor(f"xres{b}", [128, 2, QW], F32))
                acbuf = Eb(nc.sbuf_tensor(f"acbuf{b}", [128, 2, QW], F32))

                ldwb = S["ldw"].n
                vq = S["v"].n
                sq_ = S["s"].n
                tq = S["t"].n
                v_qn = {}
                s_qev, s_qsq = {}, {}
                t_q, t_qn = {}, {}
                for oc in range(DC):
                    tq += 1; t_q[oc] = tq
                    sq_ += 1; s_qev[oc] = sq_
                    sq_ += 1; s_qsq[oc] = sq_
                    if oc >= 1:
                        tq += 1; t_qn[oc - 1] = tq
                        vq += 1; v_qn[oc - 1] = vq
                tq += 1; t_qn[DC - 1] = tq
                vq += 1; v_qn[DC - 1] = vq
                vq += 1; v_mm = vq
                sq_ += 1; s_mh = sq_
                tq += 1; t_nm = tq
                vq += 1; v_bias = vq

                with nc.Block() as blk:
                    @blk.gpsimd
                    def _(g):
                        for oc in range(DC):
                            if oc >= 2:
                                g.wait_ge(S["t"].h, t_q[oc - 2])
                            g.dma_start(out=wbf[:, oc % 2], in_=wq_d[oc]).then_inc(S["ldw"].h, 16)
                            S["ldw"].n += 16

                    @blk.vector
                    def _(v):
                        for oc in range(DC):
                            v.wait_ge(S["t"].h, t_qn[oc])
                            v.reduce_max(qnrow[0:1, oc:oc + 1], PS2[0:1, 1, 0:QW],
                                         axis=AXX).then_inc(S["v"].h, 1)
                            S["v"].n += 1
                        v.memset(vdum[:], 0.0)
                        v.memset(vdum[:], 0.0)
                        v.reduce_max(qmx[0:1, b:b + 1], qnrow[0:1, :],
                                     axis=AXX).then_inc(S["v"].h, 1)
                        S["v"].n += 1
                        v.wait_ge(S["t"].h, t_nm)
                        v.tensor_copy(negm128[:, 0:1], PS2[:, 0, 0:1])
                        v.memset(vdum[:], 0.0).then_inc(S["v"].h, 1)
                        S["v"].n += 1

                    @blk.scalar
                    def _(s):
                        for oc in range(DC):
                            s.wait_ge(S["t"].h, t_q[oc])
                            s.copy(QT[:, oc, :], PS1[:, oc % 3, 0:QW]).then_inc(S["s"].h, 1)
                            S["s"].n += 1
                            if oc >= 2:
                                s.wait_ge(S["t"].h, t_qn[oc - 2])
                            s.activation(qsqb[:, oc % 2, :], QT[:, oc, :],
                                         AF.Square).then_inc(S["s"].h, 1)
                            S["s"].n += 1
                        s.wait_ge(S["v"].h, v_mm)
                        s.mul(smallf[0:1, 0:1], qmx[0:1, b:b + 1], kmx[0:1, b:b + 1])
                        s.activation(smallf[0:1, 1:2], smallf[0:1, 0:1], AF.Sqrt)
                        s.activation(smallb[0:1, 0:1], smallf[0:1, 1:2], AF.Copy,
                                     bias=-2.0, scale=-ALPHA).then_inc(S["s"].h, 1)
                        S["s"].n += 1

                    @blk.tensor
                    def _(t):
                        for oc in range(DC):
                            t.wait_ge(S["ldw"].h, ldwb + 16 * (oc + 1))
                            if oc >= 3:
                                t.wait_ge(S["s"].h, s_qev[oc - 3])
                            for c in range(DC):
                                m = t.matmul(PS1[:, oc % 3, 0:QW], wbf[:, oc % 2, c, :],
                                             xnown[:, c, b * QW:(b + 1) * QW],
                                             start=(c == 0), stop=(c == DC - 1))
                            m.then_inc(S["t"].h, 1); S["t"].n += 1
                            if oc >= 1:
                                t.wait_ge(S["s"].h, s_qsq[oc - 1])
                                if oc >= 2:
                                    t.wait_ge(S["v"].h, v_qn[oc - 2])
                                t.matmul(PS2[0:1, 1, 0:QW], ones128[:], qsqb[:, (oc - 1) % 2, :],
                                         start=True, stop=True).then_inc(S["t"].h, 1)
                                S["t"].n += 1
                        t.wait_ge(S["s"].h, s_qsq[DC - 1])
                        t.wait_ge(S["v"].h, v_qn[DC - 2])
                        t.matmul(PS2[0:1, 1, 0:QW], ones128[:], qsqb[:, (DC - 1) % 2, :],
                                 start=True, stop=True).then_inc(S["t"].h, 1)
                        S["t"].n += 1
                        t.wait_ge(S["s"].h, s_mh)
                        t.matmul(PS2[:, 0, 0:1], ones1r[:], smallb[0:1, 0:1],
                                 start=True, stop=True).then_inc(S["t"].h, 1)
                        S["t"].n += 1

                assert S["v"].n == vq and S["s"].n == sq_ and S["t"].n == tq

                NG = NKC // 2
                t_sc, t_pv, t_tr = {}, {}, {}
                s_exp = {}
                v_mk, v_rc, v_ev, v_te = {}, {}, {}, {}
                tc3, sc3, vc3 = S["t"].n, S["s"].n, S["v"].n
                for h in range(NH):
                    for kc in range(NKC):
                        tc3 += 1; t_sc[(h, kc)] = tc3
                        if kc >= 1:
                            tc3 += 1; t_pv[(h, kc - 1)] = tc3
                        if kc == 8 and h >= 1:
                            for qs in range(4):
                                tc3 += 1; t_tr[(h - 1, qs)] = tc3
                    tc3 += 1; t_pv[(h, NKC - 1)] = tc3
                    for kc in range(NKC):
                        sc3 += 1; s_exp[(h, kc)] = sc3
                    if h >= 1:
                        vc3 += 1; v_rc[h - 1] = vc3
                        for qs in range(4):
                            vc3 += 1; v_ev[(h - 1, qs)] = vc3
                    for j in range(4):
                        vc3 += 1; v_mk[(h, j)] = vc3
                    if h >= 1:
                        for qs in range(4):
                            vc3 += 1; v_te[(h - 1, qs)] = vc3
                h = NH - 1
                vc3 += 1; v_rc[h] = vc3
                for qs in range(4):
                    vc3 += 1; v_ev[(h, qs)] = vc3
                for qs in range(4):
                    tc3 += 1; t_tr[(h, qs)] = tc3
                for qs in range(4):
                    vc3 += 1; v_te[(h, qs)] = vc3

                def pvap(h, qs):
                    off = (qs % 2) * 256
                    return PS2[:, qs // 2, off:off + 129]

                with nc.Block() as blk:
                    def ptsl(kc):
                        return PT[:, kc % 4, :]

                    @blk.tensor
                    def _(t):
                        for h in range(NH):
                            for kc in range(NKC):
                                if kc >= 4:
                                    t.wait_ge(S["s"].h, s_exp[(h, kc - 4)])
                                elif h >= 1:
                                    t.wait_ge(S["s"].h, s_exp[(h - 1, NKC - 4 + kc)])
                                t.matmul(PS1[:, kc % 4, 0:QW],
                                         Ksb[:, b, kc * 128:(kc + 1) * 128],
                                         QT[:, h, :], start=True,
                                         stop=True).then_inc(S["t"].h, 1)
                                S["t"].n += 1
                                if kc >= 1:
                                    k2 = kc - 1
                                    t.wait_ge(S["s"].h, s_exp[(h, k2)])
                                    if k2 == 0 and h >= 1:
                                        t.wait_ge(S["v"].h, v_ev[(h - 1, 3)])
                                    for qs in range(4):
                                        m = t.matmul(pvap(h, qs),
                                                     ptsl(k2)[:, qs * 128:(qs + 1) * 128],
                                                     Vt[:, b, k2, :],
                                                     start=(k2 == 0 and qs % 2 == 0),
                                                     stop=(k2 == NKC - 1),
                                                     skip_group_check=True)
                                    m.then_inc(S["t"].h, 1); S["t"].n += 1
                                if kc == 8 and h >= 1:
                                    for qs in range(4):
                                        t.wait_ge(S["v"].h, v_ev[(h - 1, qs)])
                                        prev = (h - 1, qs - 2) if qs >= 2 else \
                                               ((h - 2, qs + 2) if h >= 2 else None)
                                        if prev is not None:
                                            t.wait_ge(S["v"].h, v_te[prev])
                                        t.transpose(PTRs[qs % 2][:], attok[:, qs, :],
                                                    identsb[:]).then_inc(S["t"].h, 1)
                                        S["t"].n += 1
                            k2 = NKC - 1
                            t.wait_ge(S["s"].h, s_exp[(h, k2)])
                            for qs in range(4):
                                m = t.matmul(pvap(h, qs), ptsl(k2)[:, qs * 128:(qs + 1) * 128],
                                             Vt[:, b, k2, :], start=False, stop=True,
                                             skip_group_check=True)
                            m.then_inc(S["t"].h, 1); S["t"].n += 1
                        h = NH - 1
                        for qs in range(4):
                            t.wait_ge(S["v"].h, v_ev[(h, qs)])
                            prev = (h, qs - 2) if qs >= 2 else (h - 1, qs + 2)
                            t.wait_ge(S["v"].h, v_te[prev])
                            t.transpose(PTRs[qs % 2][:], attok[:, qs, :],
                                        identsb[:]).then_inc(S["t"].h, 1)
                            S["t"].n += 1

                    @blk.scalar
                    def _(s):
                        for h in range(NH):
                            for kc in range(NKC):
                                if kc < 4:
                                    s.wait_ge(S["v"].h, v_mk[(h, kc)])
                                else:
                                    s.wait_ge(S["t"].h, t_sc[(h, kc)])
                                if kc >= 4:
                                    s.wait_ge(S["t"].h, t_pv[(h, kc - 4)])
                                elif h >= 1:
                                    s.wait_ge(S["t"].h, t_pv[(h - 1, NKC - 4 + kc)])
                                s.activation(PT[:, kc % 4, :], PS1[:, kc % 4, 0:QW],
                                             AF.Exp, bias=negm128[:, 0:1],
                                             scale=ALPHA).then_inc(S["s"].h, 1)
                                S["s"].n += 1

                    @blk.vector
                    def _(v):
                        def ev_group(h2):
                            v.wait_ge(S["t"].h, t_pv[(h2, NKC - 1)])
                            if h2 >= 1:
                                v.wait_ge(S["v"].h, v_ev[(h2 - 1, 3)])
                            for qs in range(4):
                                m = v.reciprocal(recb[:, qs:qs + 1],
                                                 pvap(h2, qs)[:, 128:129])
                            m.then_inc(S["v"].h, 1); S["v"].n += 1
                            for qs in range(4):
                                if h2 >= 1:
                                    v.wait_ge(S["t"].h, t_tr[(h2 - 1, qs)])
                                v.tensor_scalar(attok[:, qs, :], pvap(h2, qs)[:, 0:128],
                                                recb[:, qs:qs + 1], None,
                                                OP.mult).then_inc(S["v"].h, 1)
                                S["v"].n += 1

                        for h in range(NH):
                            if h >= 1:
                                ev_group(h - 1)
                            for j in range(4):
                                v.wait_ge(S["t"].h, t_sc[(h, j)])
                                v.tensor_add(PS1[:, j, 0:QW], PS1[:, j, 0:QW],
                                             masksb[:, j, :]).then_inc(S["v"].h, 1)
                                S["v"].n += 1
                            if h >= 1:
                                for qs in range(4):
                                    v.wait_ge(S["t"].h, t_tr[(h - 1, qs)])
                                    v.tensor_copy(attnT[:, h - 1, qs * 128:(qs + 1) * 128],
                                                  PTRs[qs % 2][:]).then_inc(S["v"].h, 1)
                                    S["v"].n += 1
                        h = NH - 1
                        ev_group(h)
                        for qs in range(4):
                            v.wait_ge(S["t"].h, t_tr[(h, qs)])
                            v.tensor_copy(attnT[:, h, qs * 128:(qs + 1) * 128],
                                          PTRs[qs % 2][:]).then_inc(S["v"].h, 1)
                            S["v"].n += 1

                assert S["v"].n == vc3 and S["s"].n == sc3 and S["t"].n == tc3

                if debug:
                    with nc.Block() as blk:
                        @blk.sync
                        def _(sync):
                            sync.dma_start(out=dbg["QT"][b], in_=QT[:]).then_inc(S["sto"].h, 16)
                            sync.dma_start(out=dbg["attnT"][b], in_=attnT[:]).then_inc(S["sto"].h, 16)
                            sync.dma_start(out=dbg["negm"][b, :, None], in_=negm128[:, 0:1]).then_inc(S["sto"].h, 16)
                            sync.dma_start(out=dbg["small"][b:b + 1, :], in_=smallf[:]).then_inc(S["sto"].h, 16)
                            sync.dma_start(out=dbg["smallbb"][b:b + 1, :], in_=smallb[:]).then_inc(S["sto"].h, 16)
                            sync.dma_start(out=dbg["kq"][b:b + 1, 0:B], in_=kmx[:]).then_inc(S["sto"].h, 16)
                            sync.dma_start(out=dbg["kq"][b:b + 1, B:2 * B], in_=qmx[:]).then_inc(S["sto"].h, 16)
                            sync.dma_start(out=dbg["qnrow"][b:b + 1, :], in_=qnrow[:]).then_inc(S["sto"].h, 16)
                            S["sto"].n += 16 * 8
                            sync.wait_ge(S["sto"].h, S["sto"].n)

                ldwb4 = S["ldw"].n
                ldxb4 = S["ldx"].n
                sacb = S["sac"].n
                v4, t4 = S["v"].n, S["t"].n
                v_ace = {}
                t_wo = {}
                for oc in range(DC):
                    t4 += 1; t_wo[oc] = t4
                    v4 += 1; v_ace[oc] = v4

                with nc.Block() as blk:
                    @blk.gpsimd
                    def _(g):
                        for oc in range(DC):
                            if oc >= 2:
                                g.wait_ge(S["t"].h, t_wo[oc - 2])
                            g.dma_start(out=wbf[:, oc % 2], in_=wo_d[oc]).then_inc(S["ldw"].h, 16)
                            S["ldw"].n += 16

                    @blk.sync
                    def _(sync):
                        for oc in range(DC):
                            if oc >= 2:
                                sync.wait_ge(S["v"].h, v_ace[oc - 2])
                            sync.dma_start(out=xres[:, oc % 2],
                                           in_=xown_d[b, oc]).then_inc(S["ldx"].h, 16)
                            S["ldx"].n += 16

                    @blk.vector
                    def _(v):
                        for oc in range(DC):
                            v.wait_ge(S["t"].h, t_wo[oc])
                            v.wait_ge(S["ldx"].h, ldxb4 + 16 * (oc + 1))
                            if oc >= 2:
                                v.wait_ge(S["sac"].h, sacb + 16 * (oc - 1))
                            v.tensor_add(acbuf[:, oc % 2, :], PS1[:, oc % 2, 0:QW],
                                         xres[:, oc % 2, :]).then_inc(S["v"].h, 1)
                            S["v"].n += 1

                    @blk.scalar
                    def _(s):
                        for oc in range(DC):
                            s.wait_ge(S["v"].h, v_ace[oc])
                            s.dma_start(out=attnC[b, oc * 128:(oc + 1) * 128, :],
                                        in_=acbuf[:, oc % 2, :]).then_inc(S["sac"].h, 16)
                            S["sac"].n += 16

                    @blk.tensor
                    def _(t):
                        for oc in range(DC):
                            t.wait_ge(S["ldw"].h, ldwb4 + 16 * (oc + 1))
                            if oc >= 2:
                                t.wait_ge(S["v"].h, v_ace[oc - 2])
                            for c in range(DC):
                                m = t.matmul(PS1[:, oc % 2, 0:QW], wbf[:, oc % 2, c, :],
                                             attnT[:, c, :], start=(c == 0), stop=(c == DC - 1))
                            m.then_inc(S["t"].h, 1); S["t"].n += 1

                assert S["v"].n == v4 and S["t"].n == t4

        sac_total = S["sac"].n
        if not debug:
            attn_scope.close()

        if debug:
            with nc.Block() as blk:
                @blk.sync
                def _(sync):
                    sync.dma_start(out=dbg["xnown"][:, :, :], in_=xnown[:]).then_inc(S["sto"].h, 16)
                    sync.dma_start(out=dbg["K"][:, :, :], in_=Ksb[:]).then_inc(S["sto"].h, 16)
                    sync.dma_start(out=dbg["Vt"][:, :, :, :], in_=Vt[:]).then_inc(S["sto"].h, 16)
                    S["sto"].n += 48
                    sync.wait_ge(S["sto"].h, S["sto"].n)

                @blk.gpsimd
                def _(g):
                    g.wait_ge(S["sac"].h, sac_total)
                    for bb in range(B):
                        g.dma_start(out=dbg["attnC"][bb], in_=attnC[bb]).then_inc(S["sto"].h, 16)
                        S["sto"].n += 16
                    g.wait_ge(S["sto"].h, S["sto"].n)
        if debug:
            attn_scope.close()

        with ExitStack() as p5:
            E5 = p5.enter_context
            hbuf = E5(nc.sbuf_tensor("hbuf", [128, 29, B * QW], BF16))
            xAg = E5(nc.sbuf_tensor("xAg", [32, 2, 512], BF16))
            xAu = E5(nc.sbuf_tensor("xAu", [32, 2, 512], BF16))
            dAbf = E5(nc.sbuf_tensor("dAbf", [128, ICH, 32], BF16))
            hdacc = E5(nc.sbuf_tensor("hdacc", [32, B * QW], F32))
            hdbf = E5(nc.sbuf_tensor("hdbf", [32, B * QW], BF16))

            pro_scope = ExitStack()
            Abf = pro_scope.enter_context(nc.sbuf_tensor("Abf", [128, DC, 32], BF16))
            with nc.Block() as blk:
                @blk.gpsimd
                def _(g):
                    g.dma_start(out=Abf[:], in_=gA_d[:, :, :]).then_inc(S["ldc"].h, 16)
                    g.dma_start(out=dAbf[:], in_=dA_d[:, :, :]).then_inc(S["ldc"].h, 16)
                    S["ldc"].n += 32

                @blk.vector
                def _(v):
                    v.memset(hdacc[:], 0.0)
                    v.wait_ge(S["ldc"].h, S["ldc"].n)

            for Ad, xA in ((gA_d, xAg), (uA_d, xAu)):
                tpro = S["t"].n
                with nc.Block() as blk:
                    @blk.tensor
                    def _(t):
                        for w in range(2):
                            if w == 1:
                                t.wait_ge(S["v"].h, S["v"].n + 1)
                            for c in range(DC):
                                m = t.matmul(PS1[0:32, 0, 0:QW], Abf[:, c, :],
                                             xnown[:, c, w * QW:(w + 1) * QW],
                                             start=(c == 0), stop=(c == DC - 1))
                            m.then_inc(S["t"].h, 1); S["t"].n += 1

                    @blk.vector
                    def _(v):
                        for w in range(2):
                            v.wait_ge(S["t"].h, tpro + w + 1)
                            v.tensor_scalar_mul(xA[:, w, :], PS1[0:32, 0, 0:QW],
                                                LR_SCALE).then_inc(S["v"].h, 1)
                            S["v"].n += 1

                if xA is xAg:
                    with nc.Block() as blk:
                        @blk.gpsimd
                        def _(g):
                            g.dma_start(out=Abf[:], in_=uA_d[:, :, :]).then_inc(S["ldc"].h, 16)
                            S["ldc"].n += 16

                        @blk.vector
                        def _(v):
                            v.wait_ge(S["ldc"].h, S["ldc"].n)

            pro_scope.close()

            for p in range(NPASS):
                oc0, oc1 = OCSPL[p], OCSPL[p + 1]
                nic = oc1 - oc0

                gu_scope = ExitStack()
                gbf = gu_scope.enter_context(nc.sbuf_tensor(f"gbf{p}", [128, 2, 2, DC, 128], BF16))
                brow = gu_scope.enter_context(nc.sbuf_tensor(f"brow{p}", [32, 2, 2, 128], BF16))
                t1g = gu_scope.enter_context(nc.sbuf_tensor(f"t1g{p}", [128, 2, 512], F32))
                sgb = gu_scope.enter_context(nc.sbuf_tensor(f"sgb{p}", [128, 2, 512], BF16))
                uub = gu_scope.enter_context(nc.sbuf_tensor(f"uub{p}", [128, 2, 512], BF16))
                ldwb5 = S["ldw"].n
                v5, s5, t5 = S["v"].n, S["s"].n, S["t"].n
                v_cv, v_evg, v_evu, v_hm = {}, {}, {}, {}
                s_si = {}
                t_gu, t_lr = {}, {}
                for j in range(nic):
                    for ty in range(2):
                        seq = 2 * j + ty
                        v5 += 1; v_cv[seq] = v5
                        t5 += 1; t_gu[(j, ty)] = t5
                        t5 += 1; t_lr[(j, ty)] = t5
                    v5 += 1; v_evg[j] = v5
                    s5 += 1; s_si[j] = s5
                    v5 += 1; v_evu[j] = v5
                    v5 += 1; v_hm[j] = v5

                with nc.Block() as blk:
                    @blk.gpsimd
                    def _(g):
                        for j in range(nic):
                            oc = oc0 + j
                            for ty, wd, bd in ((0, g_d, gB_d), (1, u_d, uB_d)):
                                seq = 2 * j + ty
                                if j >= 2:
                                    g.wait_ge(S["t"].h, t_lr[(j - 2, ty)])
                                g.dma_start(out=gbf[:, ty, j % 2], in_=wd[oc]).then_inc(S["ldw"].h, 16)
                                g.dma_start(out=brow[:, ty, j % 2, :],
                                            in_=bd[:, oc * 128:(oc + 1) * 128]).then_inc(S["ldw"].h, 16)
                                S["ldw"].n += 32

                    @blk.vector
                    def _(v):
                        for j in range(nic):
                            for ty in range(2):
                                seq = 2 * j + ty
                                v.wait_ge(S["ldw"].h, ldwb5 + 32 * (seq + 1))
                                ga = gbf[:, ty, j % 2].rearrange("p c o -> p (c o)")
                                v.tensor_scalar_add(ga, ga, Q_OFF).then_inc(S["v"].h, 1)
                                S["v"].n += 1
                            v.wait_ge(S["t"].h, t_lr[(j, 0)])
                            if j >= 1:
                                v.wait_ge(S["s"].h, s_si[j - 1])
                            sb = sgates[:, oc0 + j:oc0 + j + 1].broadcast_to([128, QW])
                            for w in range(2):
                                v.tensor_mul(t1g[:, w, :], PS1[:, w, :QW], sb)
                            v.tensor_add(t1g[:, 0, :], t1g[:, 0, :], PS2[:, 0, :QW])
                            v.tensor_add(t1g[:, 1, :], t1g[:, 1, :],
                                         PS2[:, 1, :QW]).then_inc(S["v"].h, 1)
                            S["v"].n += 1
                            v.wait_ge(S["t"].h, t_lr[(j, 1)])
                            ub = sups[:, oc0 + j:oc0 + j + 1].broadcast_to([128, QW])
                            v.tensor_mul(uub[:, 0, :], PS1[:, 2, :QW], ub)
                            v.tensor_mul(uub[:, 1, :], PS1[:, 3, 0:QW], ub)
                            v.tensor_add(uub[:, 0, :], uub[:, 0, :], PS2[:, 0, :QW])
                            v.tensor_add(uub[:, 1, :], uub[:, 1, :],
                                         PS2[:, 1, :QW]).then_inc(S["v"].h, 1)
                            S["v"].n += 1
                            v.wait_ge(S["s"].h, s_si[j])
                            v.tensor_mul(hbuf[:, j, :].rearrange("p (w q) -> p w q", w=2),
                                         sgb[:, :, :], uub[:, :, :]).then_inc(S["v"].h, 1)
                            S["v"].n += 1

                    @blk.scalar
                    def _(s):
                        for j in range(nic):
                            s.wait_ge(S["v"].h, v_evg[j])
                            s.activation(sgb[:].rearrange("p w q -> p (w q)"),
                                         t1g[:].rearrange("p w q -> p (w q)"),
                                         AF.Silu).then_inc(S["s"].h, 1)
                            S["s"].n += 1

                    @blk.tensor
                    def _(t):
                        for j in range(nic):
                            for ty in range(2):
                                seq = 2 * j + ty
                                t.wait_ge(S["v"].h, v_cv[seq])
                                if j >= 1:
                                    t.wait_ge(S["v"].h, (v_evg if ty == 0 else v_evu)[j - 1])
                                for c in range(DC):
                                    for w in range(2):
                                        dstp = (PS1[:, w, 0:QW] if ty == 0 else
                                                (PS1[:, 2, 0:QW] if w == 0 else PS1[:, 3, 0:QW]))
                                        m = t.matmul(dstp, gbf[:, ty, j % 2, c, :],
                                                     xnown[:, c, w * QW:(w + 1) * QW],
                                                     start=(c == 0), stop=(c == DC - 1))
                                m.then_inc(S["t"].h, 1); S["t"].n += 1
                                xA = xAg if ty == 0 else xAu
                                if ty == 0 and j >= 1:
                                    t.wait_ge(S["v"].h, v_evu[j - 1])
                                if ty == 1:
                                    t.wait_ge(S["v"].h, v_evg[j])
                                for w in range(2):
                                    m = t.matmul(PS2[:, w, 0:QW], brow[:, ty, j % 2, :], xA[:, w, :],
                                                 start=True, stop=True)
                                m.then_inc(S["t"].h, 1); S["t"].n += 1

                assert S["v"].n == v5 and S["s"].n == s5 and S["t"].n == t5

                gu_scope.close()

                if p == NPASS - 1:
                    dB_scope = ExitStack()
                    dBbf = dB_scope.enter_context(nc.sbuf_tensor("dBbf", [32, D], BF16))
                thd = S["t"].n
                with nc.Block() as blk:
                    if p == NPASS - 1:
                        @blk.gpsimd
                        def _(g):
                            g.dma_start(out=dBbf[:], in_=dB_d[:, :]).then_inc(S["ldc"].h, 16)
                            S["ldc"].n += 16

                    @blk.tensor
                    def _(t):
                        for w in range(2):
                            if w == 1:
                                t.wait_ge(S["v"].h, S["v"].n + 1)
                            for j in range(nic):
                                m = t.matmul(PS2[0:32, 0, 0:QW], dAbf[:, oc0 + j, :],
                                             hbuf[:, j, w * QW:(w + 1) * QW],
                                             start=(j == 0), stop=(j == nic - 1))
                            m.then_inc(S["t"].h, 1); S["t"].n += 1

                    @blk.vector
                    def _(v):
                        for w in range(2):
                            v.wait_ge(S["t"].h, thd + w + 1)
                            v.tensor_add(hdacc[:, w * QW:(w + 1) * QW],
                                         hdacc[:, w * QW:(w + 1) * QW],
                                         PS2[0:32, 0, 0:QW]).then_inc(S["v"].h, 1)
                            S["v"].n += 1
                        if p == NPASS - 1:
                            v.memset(vdum[:], 0.0)
                            v.tensor_scalar_mul(hdbf[:], hdacc[:], LR_SCALE)

                dn_scope = ExitStack()
                dbfb = dn_scope.enter_context(nc.sbuf_tensor(f"dbfb{p}", [128, 2, 29, 128], BF16))
                pstage = dn_scope.enter_context(nc.sbuf_tensor(f"pstage{p}", [128, 2, B * QW], F32))
                ac0 = dn_scope.enter_context(nc.sbuf_tensor(f"ac0_{p}", [128, 2, QW], F32))
                ac1 = dn_scope.enter_context(nc.sbuf_tensor(f"ac1_{p}", [128, 2, QW], F32))
                t1d = dn_scope.enter_context(nc.sbuf_tensor(f"t1d{p}", [128, 2, 2, 512], F32))
                ldwbD = S["ldw"].n
                pldb = S["pld"].n
                sppb = S["spp"].n
                stob = S["sto"].n
                last = (p == NPASS - 1)
                per_pld = (16 if p > 0 else 0) + (32 if last else 0)
                vD, tD = S["v"].n, S["t"].n
                v_dcv, v_dev = {}, {}
                t_dn, t_dlr = {}, {}
                for Hc in range(DC):
                    vD += 1; v_dcv[Hc] = vD
                    tD += 1; t_dn[(Hc, 0)] = tD
                    tD += 1; t_dn[(Hc, 1)] = tD
                    if last:
                        tD += 1; t_dlr[Hc] = tD
                    if Hc >= 1:
                        vD += 1; v_dev[Hc - 1] = vD
                vD += 1; v_dev[DC - 1] = vD

                with nc.Block() as blk:
                    @blk.gpsimd
                    def _(g):
                        for Hc in range(DC):
                            if Hc >= 2:
                                g.wait_ge(S["t"].h, t_dn[(Hc - 2, 1)])
                            g.dma_start(out=dbfb[:, Hc % 2, 0:nic, :],
                                        in_=dn_d[Hc, :, oc0:oc1, :]).then_inc(S["ldw"].h, 16)
                            S["ldw"].n += 16

                    @blk.sync
                    def _(sync):
                        for Hc in range(DC):
                            if per_pld:
                                if Hc >= 2:
                                    sync.wait_ge(S["v"].h, v_dev[Hc - 2])
                                if last and Hc == 0:
                                    sync.wait_ge(S["sac"].h, sac_total)
                                if p > 0 and Hc == 0:
                                    sync.wait_ge(S["spp"].h, sppb)
                                if p > 0:
                                    sync.dma_start(out=pstage[:, Hc % 2, :],
                                                   in_=partial[Hc]).then_inc(S["pld"].h, 16)
                                    S["pld"].n += 16
                                if last:
                                    sync.dma_start(out=ac0[:, Hc % 2, :],
                                                   in_=attnC[0, Hc * 128:(Hc + 1) * 128, :]).then_inc(S["pld"].h, 16)
                                    sync.dma_start(out=ac1[:, Hc % 2, :],
                                                   in_=attnC[1, Hc * 128:(Hc + 1) * 128, :]).then_inc(S["pld"].h, 16)
                                    S["pld"].n += 32

                    @blk.vector
                    def _(v):
                        def devac(Hc):
                            if per_pld:
                                v.wait_ge(S["pld"].h, pldb + per_pld * (Hc + 1))
                            if Hc >= 2:
                                if last:
                                    v.wait_ge(S["sto"].h, stob + 32 * (Hc - 1))
                                else:
                                    v.wait_ge(S["spp"].h, sppb + 16 * (Hc - 1))
                            v.wait_ge(S["t"].h, t_dlr[Hc] if last else t_dn[(Hc, 1)])
                            sdb = sdowns[:, Hc:Hc + 1].broadcast_to([128, QW])
                            for w in range(2):
                                dst = t1d[:, Hc % 2, w, :]
                                if Hc % 2 == 0:
                                    srcp = PS1[:, w, 0:QW]
                                else:
                                    srcp = PS1[:, 2, 0:QW] if w == 0 else PS1[:, 3, 0:QW]
                                if p == 0:
                                    v.tensor_mul(dst, srcp, sdb)
                                else:
                                    v.scalar_tensor_tensor(dst, srcp,
                                                           sdowns[:, Hc:Hc + 1],
                                                           pstage[:, Hc % 2, w * QW:(w + 1) * QW],
                                                           OP.mult, OP.add)
                            if last:
                                for w in range(2):
                                    v.tensor_add(t1d[:, Hc % 2, w, :], t1d[:, Hc % 2, w, :],
                                                 PS2[:, w, 0:QW])
                                v.tensor_add(t1d[:, Hc % 2, 0, :], t1d[:, Hc % 2, 0, :],
                                             ac0[:, Hc % 2, :])
                                v.tensor_add(t1d[:, Hc % 2, 1, :], t1d[:, Hc % 2, 1, :],
                                             ac1[:, Hc % 2, :]).then_inc(S["v"].h, 1)
                            else:
                                v.memset(vdum[:], 0.0).then_inc(S["v"].h, 1)
                            S["v"].n += 1

                        for Hc in range(DC):
                            v.wait_ge(S["ldw"].h, ldwbD + 16 * (Hc + 1))
                            ap = dbfb[:, Hc % 2, 0:nic, :].rearrange("p c o -> p (c o)")
                            v.tensor_scalar_add(ap, ap, Q_OFF).then_inc(S["v"].h, 1)
                            S["v"].n += 1
                            if Hc >= 1:
                                devac(Hc - 1)
                        devac(DC - 1)

                    @blk.scalar
                    def _(sg):
                        for Hc in range(DC):
                            sg.wait_ge(S["v"].h, v_dev[Hc])
                            if not last:
                                sg.dma_start(out=partial[Hc],
                                             in_=t1d[:, Hc % 2].rearrange("p w q -> p (w q)")).then_inc(S["spp"].h, 16)
                                S["spp"].n += 16
                            else:
                                sg.dma_start(out=youts[0][Hc * 128:(Hc + 1) * 128, :],
                                             in_=t1d[:, Hc % 2, 0, :]).then_inc(S["sto"].h, 16)
                                sg.dma_start(out=youts[1][Hc * 128:(Hc + 1) * 128, :],
                                             in_=t1d[:, Hc % 2, 1, :]).then_inc(S["sto"].h, 16)
                                S["sto"].n += 32

                    @blk.tensor
                    def _(t):
                        for Hc in range(DC):
                            t.wait_ge(S["v"].h, v_dcv[Hc])
                            if Hc >= 2:
                                t.wait_ge(S["v"].h, v_dev[Hc - 2])
                            for j in range(nic):
                                for w in range(2):
                                    if Hc % 2 == 0:
                                        dstp = PS1[:, w, 0:QW]
                                    else:
                                        dstp = PS1[:, 2, 0:QW] if w == 0 else PS1[:, 3, 0:QW]
                                    m = t.matmul(dstp, dbfb[:, Hc % 2, j, :],
                                                 hbuf[:, j, w * QW:(w + 1) * QW],
                                                 start=(j == 0), stop=(j == nic - 1))
                                    if j == nic - 1:
                                        m.then_inc(S["t"].h, 1); S["t"].n += 1
                            if last:
                                if Hc == 0:
                                    t.wait_ge(S["ldc"].h, S["ldc"].n)
                                if Hc >= 1:
                                    t.wait_ge(S["v"].h, v_dev[Hc - 1])
                                for w in range(2):
                                    m = t.matmul(PS2[:, w, 0:QW],
                                                 dBbf[:, Hc * 128:(Hc + 1) * 128],
                                                 hdbf[:, w * QW:(w + 1) * QW],
                                                 start=True, stop=True)
                                m.then_inc(S["t"].h, 1); S["t"].n += 1

                assert S["v"].n == vD and S["t"].n == tD
                dn_scope.close()
                if p == NPASS - 1:
                    dB_scope.close()

            if debug:
                with nc.Block() as blk:
                    @blk.sync
                    def _(sync):
                        sync.dma_start(out=dbg["hbuf"][:, :, :], in_=hbuf[:, 0:28, :]).then_inc(S["sto"].h, 16)
                        sync.dma_start(out=dbg["xA"][:, 0:2, :], in_=xAg[:]).then_inc(S["sto"].h, 16)
                        sync.dma_start(out=dbg["xA"][:, 2:4, :], in_=xAu[:]).then_inc(S["sto"].h, 16)
                        sync.dma_start(out=dbg["hd"][:, :], in_=hdacc[:]).then_inc(S["sto"].h, 16)
                        S["sto"].n += 64
                        sync.wait_ge(S["sto"].h, S["sto"].n)

            with nc.Block() as blk:
                @blk.sync
                def _(sync):
                    sync.wait_ge(S["sto"].h, S["sto"].n)

    return nc


_CACHE = {}


def _prep_shared(inputs):
    d = {}
    wq = np.ascontiguousarray(inputs["L0_attn_wq"], dtype=np.float32)
    wo = np.ascontiguousarray(inputs["L0_attn_wo"], dtype=np.float32)
    d["wq_blk"] = np.ascontiguousarray(wq.reshape(DC, 128, DC, 128).transpose(2, 1, 0, 3))
    d["wo_blk"] = np.ascontiguousarray(wo.reshape(DC, 128, DC, 128).transpose(2, 1, 0, 3))
    d["wk"] = np.ascontiguousarray(inputs["L0_attn_wk"], dtype=np.float32)
    d["wv"] = np.ascontiguousarray(inputs["L0_attn_wv"], dtype=np.float32)
    gq = np.ascontiguousarray(inputs["L0_ffn_gate_q"].T)
    uq = np.ascontiguousarray(inputs["L0_ffn_up_q"].T)
    dq = np.ascontiguousarray(inputs["L0_ffn_down_q"].T)
    d["gate_blk"] = np.ascontiguousarray(gq.reshape(DC, 128, ICH, 128).transpose(2, 1, 0, 3))
    d["up_blk"] = np.ascontiguousarray(uq.reshape(DC, 128, ICH, 128).transpose(2, 1, 0, 3))
    d["down_blk"] = np.ascontiguousarray(dq.reshape(ICH, 128, DC, 128).transpose(2, 1, 0, 3))
    d["gsc"] = np.ascontiguousarray(np.asarray(inputs["L0_ffn_gate_scale"], dtype=np.float32).reshape(ICH, 128).T)
    d["usc"] = np.ascontiguousarray(np.asarray(inputs["L0_ffn_up_scale"], dtype=np.float32).reshape(ICH, 128).T)
    d["dsc"] = np.ascontiguousarray(np.asarray(inputs["L0_ffn_down_scale"], dtype=np.float32).reshape(DC, 128).T)
    for k, name in (("gB", "L0_ffn_gate_B"), ("uB", "L0_ffn_up_B"),
                    ("dB", "L0_ffn_down_B")):
        d[k] = np.ascontiguousarray(inputs[name], dtype=np.float32)
    d["gA"] = np.ascontiguousarray(np.asarray(inputs["L0_ffn_gate_A"], dtype=np.float32)
                                   .reshape(DC, 128, 32).transpose(1, 0, 2))
    d["uA"] = np.ascontiguousarray(np.asarray(inputs["L0_ffn_up_A"], dtype=np.float32)
                                   .reshape(DC, 128, 32).transpose(1, 0, 2))
    d["dA"] = np.ascontiguousarray(np.asarray(inputs["L0_ffn_down_A"], dtype=np.float32)
                                   .reshape(ICH, 128, 32).transpose(1, 0, 2))
    return d


def _prep_core(x, c):
    klen = QW * (c + 1)
    xkv = np.zeros((B, D, T), dtype=np.float32)
    for b in range(B):
        xt = x[b].T
        xkv[b, :, 0:QW] = xt[:, QW * c:QW * (c + 1)]
        if c > 0:
            xkv[b, :, QW:klen] = xt[:, 0:QW * c]
    xb = np.ascontiguousarray(
        xkv.reshape(B, DC, 128, NTT, TTW).transpose(0, 3, 2, 1, 4))
    xo = np.ascontiguousarray(
        xkv[:, :, 0:QW].reshape(B, DC, 128, QW))
    vones = np.zeros((B, NKC), dtype=np.float32)
    vones[:, 0:klen // 128] = 1.0
    return {"xkv": xb, "xown": xo, "vones": vones}


def kernel(**inputs):
    if "nc" not in _CACHE:
        _CACHE["nc"] = build_program(debug=False)
    nc = _CACHE["nc"]
    shared = _prep_shared(inputs)
    x = np.ascontiguousarray(np.asarray(inputs["L0_attn_x"], dtype=np.float32))
    in_maps = []
    for c in range(8):
        m = dict(shared)
        m.update(_prep_core(x, c))
        in_maps.append(m)
    res = run_bass_kernel_spmd(nc, in_maps, core_ids=list(range(8)))
    out = np.empty((B, T, D), dtype=np.float32)
    for c in range(8):
        r = res.results[c]
        out[0, QW * c:QW * (c + 1), :] = np.asarray(r["y0"]).T
        out[1, QW * c:QW * (c + 1), :] = np.asarray(r["y1"]).T
    return out


# version 13
# speedup vs baseline: 1.0654x; 1.0654x over previous
import numpy as np
from contextlib import ExitStack

import concourse.bass as bass
import concourse.mybir as mybir
from concourse.bass_utils import run_bass_kernel_spmd

F32 = mybir.dt.float32
BF16 = mybir.dt.bfloat16
I32 = mybir.dt.int32
AF = mybir.ActivationFunctionType
OP = mybir.AluOpType
AXX = mybir.AxisListType.X

B = 2
T = 4096
D = 4096
DC = 32
NH = 32
I = 11008
ICH = 86
QW = 512
NKC = 32
TTW = 256
NTT = T // TTW
NPASS = 3
OCSPL = [0, 29, 58, ICH]
ALPHA = 1.0 / np.sqrt(128.0)
Q_OFF = -7.5
Q_STEP = 0.005
LR_SCALE = 0.02
NEG = -30000.0


class Sem:
    def __init__(self, h):
        self.h = h
        self.n = 0


def build_program(debug=False):
    nc = bass.Bass("TRN2", target_bir_lowering=False)

    xkv = nc.dram_tensor("xkv", [B, NTT, 128, DC, TTW], F32, kind="ExternalInput").ap()
    xown_d = nc.dram_tensor("xown", [B, DC, 128, QW], F32, kind="ExternalInput").ap()
    wk_d = nc.dram_tensor("wk", [D, 128], F32, kind="ExternalInput").ap()
    wv_d = nc.dram_tensor("wv", [D, 128], F32, kind="ExternalInput").ap()
    wq_d = nc.dram_tensor("wq_blk", [DC, 128, DC, 128], F32, kind="ExternalInput").ap()
    wo_d = nc.dram_tensor("wo_blk", [DC, 128, DC, 128], F32, kind="ExternalInput").ap()
    g_d = nc.dram_tensor("gate_blk", [ICH, 128, DC, 128], I32, kind="ExternalInput").ap()
    u_d = nc.dram_tensor("up_blk", [ICH, 128, DC, 128], I32, kind="ExternalInput").ap()
    dn_d = nc.dram_tensor("down_blk", [DC, 128, ICH, 128], I32, kind="ExternalInput").ap()
    gsc_d = nc.dram_tensor("gsc", [128, ICH], F32, kind="ExternalInput").ap()
    usc_d = nc.dram_tensor("usc", [128, ICH], F32, kind="ExternalInput").ap()
    dsc_d = nc.dram_tensor("dsc", [128, DC], F32, kind="ExternalInput").ap()
    gA_d = nc.dram_tensor("gA", [128, DC, 32], F32, kind="ExternalInput").ap()
    gB_d = nc.dram_tensor("gB", [32, I], F32, kind="ExternalInput").ap()
    uA_d = nc.dram_tensor("uA", [128, DC, 32], F32, kind="ExternalInput").ap()
    uB_d = nc.dram_tensor("uB", [32, I], F32, kind="ExternalInput").ap()
    dA_d = nc.dram_tensor("dA", [128, ICH, 32], F32, kind="ExternalInput").ap()
    dB_d = nc.dram_tensor("dB", [32, D], F32, kind="ExternalInput").ap()
    vones_d = nc.dram_tensor("vones", [B, NKC], F32, kind="ExternalInput").ap()

    attnC = nc.dram_tensor("attnC", [B, D, QW], F32, kind="Internal").ap()
    partial = nc.dram_tensor("partial", [DC, 128, B * QW], F32, kind="Internal").ap()

    youts = [nc.dram_tensor(f"y{b}", [D, QW], F32, kind="ExternalOutput").ap()
             for b in range(B)]

    jj, pp, qq = np.meshgrid(np.arange(4), np.arange(128), np.arange(QW), indexing="ij")
    import ml_dtypes as _mld
    tri_np = np.where(jj * 128 + pp <= qq, 0.0, NEG).astype(_mld.bfloat16)
    tri_d = nc.inline_tensor(tri_np, name="trimask").ap()
    import ml_dtypes
    ident_d = nc.inline_tensor(np.eye(128, dtype=ml_dtypes.bfloat16), name="ident").ap()

    dbg = {}
    if debug:
        dbg["xnown"] = nc.dram_tensor("dbg_xnown", [128, DC, B * QW], BF16, kind="ExternalOutput").ap()
        dbg["K"] = nc.dram_tensor("dbg_K", [128, B, T], BF16, kind="ExternalOutput").ap()
        dbg["Vt"] = nc.dram_tensor("dbg_Vt", [128, B, NKC, 129], BF16, kind="ExternalOutput").ap()
        dbg["QT"] = nc.dram_tensor("dbg_QT", [B, 128, NH, QW], BF16, kind="ExternalOutput").ap()
        dbg["attnT"] = nc.dram_tensor("dbg_attnT", [B, 128, NH, QW], BF16, kind="ExternalOutput").ap()
        dbg["negm"] = nc.dram_tensor("dbg_negm", [B, 128, 1], F32, kind="ExternalOutput").ap()
        dbg["small"] = nc.dram_tensor("dbg_small", [B, 16], F32, kind="ExternalOutput").ap()
        dbg["smallbb"] = nc.dram_tensor("dbg_smallbb", [B, 16], BF16, kind="ExternalOutput").ap()
        dbg["kq"] = nc.dram_tensor("dbg_kq", [B, 2 * B], F32, kind="ExternalOutput").ap()
        dbg["qnrow"] = nc.dram_tensor("dbg_qnrow", [B, NH], F32, kind="ExternalOutput").ap()
        dbg["hbuf"] = nc.dram_tensor("dbg_hbuf", [128, 28, B * QW], BF16, kind="ExternalOutput").ap()
        dbg["xA"] = nc.dram_tensor("dbg_xA", [32, 4, QW], BF16, kind="ExternalOutput").ap()
        dbg["hd"] = nc.dram_tensor("dbg_hd", [32, B * QW], F32, kind="ExternalOutput").ap()
        dbg["attnC"] = nc.dram_tensor("dbg_attnC", [B, D, QW], F32, kind="ExternalOutput").ap()

    with ExitStack() as ctx:
        E = ctx.enter_context
        xnown = E(nc.sbuf_tensor("xnown", [128, DC, B * QW], BF16))
        ones128 = E(nc.sbuf_tensor("ones128", [128, 1], BF16))
        ones1r = E(nc.sbuf_tensor("ones1r", [1, 128], BF16))
        vonesrow = E(nc.sbuf_tensor("vonesrow", [1, B * NKC], BF16))
        sgates = E(nc.sbuf_tensor("sgates", [128, ICH], F32))
        sups = E(nc.sbuf_tensor("sups", [128, ICH], F32))
        sdowns = E(nc.sbuf_tensor("sdowns", [128, DC], F32))
        kmx = E(nc.sbuf_tensor("kmx", [1, B], F32))
        qmx = E(nc.sbuf_tensor("qmx", [1, B], F32))
        smallf = E(nc.sbuf_tensor("smallf", [1, 16], F32))
        smallb = E(nc.sbuf_tensor("smallb", [1, 16], BF16))
        negm128 = E(nc.sbuf_tensor("negm128", [128, 1], F32))
        vdum = E(nc.sbuf_tensor("vdum", [1, 8], F32))

        PS1 = E(nc.psum_tensor("PS1", [128, 4, 512], F32))
        PS2 = E(nc.psum_tensor("PS2", [128, 2, 512], F32))
        PTRs = [E(nc.psum_tensor("PTRa", [128, 128], BF16)),
                E(nc.psum_tensor("PTRb", [128, 128], BF16))]

        names = ["ldw", "ldx", "ldc", "v", "s", "t", "sac", "spp", "pld", "sto"]
        S = {n: Sem(E(nc.semaphore(n))) for n in names}

        attn_scope = ExitStack()
        Ksb = attn_scope.enter_context(nc.sbuf_tensor("Ksb", [128, B, T], BF16))
        Vt = attn_scope.enter_context(nc.sbuf_tensor("Vt", [128, B, NKC, 129], BF16))
        masksb = attn_scope.enter_context(nc.sbuf_tensor("masksb", [128, 4, QW], BF16))
        identsb = attn_scope.enter_context(nc.sbuf_tensor("identsb", [128, 128], BF16))

        with nc.Block() as blk:
            @blk.sync
            def _(sync):
                sync.dma_start(out=masksb[:], in_=tri_d.rearrange("j p q -> p j q")).then_inc(S["ldc"].h, 16)
                sync.dma_start(out=identsb[:], in_=ident_d[:, :]).then_inc(S["ldc"].h, 16)
                sync.dma_start(out=sgates[:], in_=gsc_d[:, :]).then_inc(S["ldc"].h, 16)
                sync.dma_start(out=sups[:], in_=usc_d[:, :]).then_inc(S["ldc"].h, 16)
                sync.dma_start(out=sdowns[:], in_=dsc_d[:, :]).then_inc(S["ldc"].h, 16)
                S["ldc"].n += 5 * 16

            v0 = S["v"].n + 1
            t0 = S["t"].n + 1

            @blk.gpsimd
            def _(g):
                g.dma_start(out=vonesrow[0:1, :], in_=vones_d[None, :, :].rearrange("o b k -> o (b k)")).then_inc(S["ldc"].h, 16)
                S["ldc"].n += 16

            @blk.vector
            def _(v):
                v.wait_ge(S["ldc"].h, S["ldc"].n)
                v.memset(ones128[:], 1.0)
                v.memset(ones1r[:], 1.0)
                v.tensor_scalar_mul(sgates[:], sgates[:], Q_STEP)
                v.tensor_scalar_mul(sups[:], sups[:], Q_STEP)
                v.tensor_scalar_mul(sdowns[:], sdowns[:], Q_STEP).then_inc(S["v"].h, 1)
                S["v"].n += 1
                v.wait_ge(S["t"].h, t0)
                v.tensor_copy(Vt[:, 0, :, 128], PS2[:, 0, 0:NKC])
                v.tensor_copy(Vt[:, 1, :, 128], PS2[:, 0, NKC:2 * NKC]).then_inc(S["v"].h, 1)
                S["v"].n += 1

            @blk.tensor
            def _(t):
                t.wait_ge(S["v"].h, v0)
                t.matmul(PS2[:, 0, 0:B * NKC], ones1r[:], vonesrow[:],
                         start=True, stop=True).then_inc(S["t"].h, 1)
                S["t"].n += 1

        with ExitStack() as p1:
            E1 = p1.enter_context
            wkb = E1(nc.sbuf_tensor("wkb", [128, DC, 128], BF16))
            wvb = E1(nc.sbuf_tensor("wvb", [128, DC, 128], BF16))
            xbb = E1(nc.sbuf_tensor("xbb", [128, 2, DC, TTW], BF16))
            xsqb = E1(nc.sbuf_tensor("xsqb", [128, 2, DC, TTW], BF16))
            rowsf = E1(nc.sbuf_tensor("rowsf", [1, 4, TTW], F32))
            rowsb = E1(nc.sbuf_tensor("rowsb", [1, 2, TTW], BF16))
            bcb = E1(nc.sbuf_tensor("bcb", [128, 2, 2, TTW], BF16))
            ksq = E1(nc.sbuf_tensor("ksq", [128, TTW], BF16))
            knrow = E1(nc.sbuf_tensor("knrow", [1, NTT], F32))

            with nc.Block() as blk:
                @blk.gpsimd
                def _(g):
                    g.dma_start(out=wkb[:], in_=wk_d.rearrange("(c p) h -> p c h", p=128)).then_inc(S["ldc"].h, 16)
                    g.dma_start(out=wvb[:], in_=wv_d.rearrange("(c p) h -> p c h", p=128)).then_inc(S["ldc"].h, 16)
                    S["ldc"].n += 32

                @blk.vector
                def _(v):
                    v.wait_ge(S["ldc"].h, S["ldc"].n)

            tiles = [(b, tt) for b in range(B) for tt in range(NTT)]
            NTL = len(tiles)

            vb, sb_, tb = S["v"].n, S["s"].n, S["t"].n
            ldxb = S["ldx"].n
            v_sread, v_rows, v_norm, v_kvev, v_kn = {}, {}, {}, {}, {}
            v_bcb = {}
            s_xsq, s_rstd, s_ksq = {}, {}, {}
            t_stats, t_bc, t_kv, t_kn = {}, {}, {}, {}
            vc, sc_, tc = vb, sb_, tb
            for i, (b, tt) in enumerate(tiles):
                sc_ += 1; s_xsq[i] = sc_
                tc += 1; t_stats[i] = tc
                if i >= 1:
                    tc += 1; t_kv[i - 1] = tc
                vc += 1; v_sread[i] = vc
                sc_ += 1; s_rstd[i] = sc_
                if i >= 1:
                    sc_ += 1; s_ksq[i - 1] = sc_
                vc += 1; v_rows[i] = vc
                tc += 1; t_bc[i] = tc
                if i >= 1:
                    tc += 1; t_kn[i - 1] = tc
                vc += 1; v_bcb[i] = vc
                if i >= 1:
                    vc += 1; v_kvev[i - 1] = vc
                vc += 1; v_norm[i] = vc
                if i >= 1:
                    vc += 1; v_kn[i - 1] = vc
                    if tiles[i - 1][1] == NTT - 1:
                        vc += 1
            tc += 1; t_kv[NTL - 1] = tc
            tc += 1; t_kn[NTL - 1] = tc
            sc_ += 1; s_ksq[NTL - 1] = sc_
            vc += 1; v_kvev[NTL - 1] = vc
            vc += 1; v_kn[NTL - 1] = vc
            vc += 1

            with nc.Block() as blk:
                @blk.gpsimd
                def _(g):
                    for i, (b, tt) in enumerate(tiles):
                        if i >= 2:
                            g.wait_ge(S["t"].h, t_kv[i - 2])
                        g.dma_start(out=xbb[:, i % 2], in_=xkv[b, tt]).then_inc(S["ldx"].h, 16)
                        S["ldx"].n += 16

                @blk.scalar
                def _(s):
                    for i, (b, tt) in enumerate(tiles):
                        s.wait_ge(S["ldx"].h, ldxb + 16 * (i + 1))
                        if i >= 2:
                            s.wait_ge(S["t"].h, t_stats[i - 2])
                        s.activation(xsqb[:, i % 2].rearrange("p c t -> p (c t)"),
                                     xbb[:, i % 2].rearrange("p c t -> p (c t)"),
                                     AF.Square).then_inc(S["s"].h, 1)
                        S["s"].n += 1
                        s.wait_ge(S["v"].h, v_sread[i])
                        s.activation(rowsf[0:1, 3, :], rowsf[0:1, 2, :], AF.Sqrt).then_inc(S["s"].h, 1)
                        S["s"].n += 1
                        if i >= 1:
                            b2, tt2 = tiles[i - 1]
                            s.wait_ge(S["v"].h, v_kvev[i - 1])
                            s.activation(ksq[:], Ksb[:, b2, tt2 * TTW:(tt2 + 1) * TTW],
                                         AF.Square).then_inc(S["s"].h, 1)
                            S["s"].n += 1
                    b2, tt2 = tiles[NTL - 1]
                    s.wait_ge(S["v"].h, v_kvev[NTL - 1])
                    s.activation(ksq[:], Ksb[:, b2, tt2 * TTW:(tt2 + 1) * TTW],
                                 AF.Square).then_inc(S["s"].h, 1)
                    S["s"].n += 1

                @blk.tensor
                def _(t):
                    def kv_group(i2):
                        t.wait_ge(S["v"].h, v_norm[i2])
                        if i2 >= 1:
                            t.wait_ge(S["v"].h, v_kvev[i2 - 1])
                        for c in range(DC):
                            t.matmul(PS1[:, 0, 0:TTW], wkb[:, c, :], xbb[:, i2 % 2, c, :],
                                     start=(c == 0), stop=(c == DC - 1))
                        for sub in range(2):
                            for c in range(DC):
                                m = t.matmul(PS1[:, 1, sub * 128:(sub + 1) * 128],
                                             xbb[:, i2 % 2, c, sub * 128:(sub + 1) * 128],
                                             wvb[:, c, :],
                                             start=(c == 0), stop=(c == DC - 1))
                        m.then_inc(S["t"].h, 1); S["t"].n += 1

                    def kn_group(i2):
                        t.wait_ge(S["s"].h, s_ksq[i2])
                        if i2 >= 1:
                            t.wait_ge(S["v"].h, v_kn[i2 - 1])
                        t.matmul(PS1[0:1, 2, 0:TTW], ones128[:], ksq[:],
                                 start=True, stop=True).then_inc(S["t"].h, 1)
                        S["t"].n += 1

                    for i, (b, tt) in enumerate(tiles):
                        t.wait_ge(S["s"].h, s_xsq[i])
                        if i >= 1:
                            t.wait_ge(S["v"].h, v_bcb[i - 1])
                        for c in range(DC):
                            t.matmul(PS2[0:1, 0, 0:TTW], ones128[:], xbb[:, i % 2, c, :],
                                     start=(c == 0), stop=(c == DC - 1))
                            m = t.matmul(PS2[0:1, 1, 0:TTW], ones128[:], xsqb[:, i % 2, c, :],
                                         start=(c == 0), stop=(c == DC - 1))
                        m.then_inc(S["t"].h, 1); S["t"].n += 1
                        if i >= 1:
                            kv_group(i - 1)
                        t.wait_ge(S["v"].h, v_rows[i])
                        t.matmul(PS2[:, 0, 0:TTW], ones1r[:], rowsb[0:1, 0, :], start=True, stop=True)
                        t.matmul(PS2[:, 1, 0:TTW], ones1r[:], rowsb[0:1, 1, :],
                                 start=True, stop=True).then_inc(S["t"].h, 1)
                        S["t"].n += 1
                        if i >= 1:
                            kn_group(i - 1)
                    kv_group(NTL - 1)
                    kn_group(NTL - 1)

                @blk.vector
                def _(v):
                    for i, (b, tt) in enumerate(tiles):
                        v.wait_ge(S["t"].h, t_stats[i])
                        v.tensor_scalar_mul(rowsf[0:1, 0, :], PS2[0:1, 0, 0:TTW], 1.0 / D)
                        v.tensor_scalar_mul(rowsf[0:1, 1, :], PS2[0:1, 1, 0:TTW], 1.0 / D)
                        v.tensor_mul(rowsf[0:1, 2, :], rowsf[0:1, 0, :], rowsf[0:1, 0, :])
                        v.tensor_sub(rowsf[0:1, 1, :], rowsf[0:1, 1, :], rowsf[0:1, 2, :])
                        v.tensor_scalar_add(rowsf[0:1, 1, :], rowsf[0:1, 1, :], 1e-5)
                        v.reciprocal(rowsf[0:1, 2, :], rowsf[0:1, 1, :]).then_inc(S["v"].h, 1)
                        S["v"].n += 1
                        v.wait_ge(S["s"].h, s_rstd[i])
                        v.tensor_mul(rowsf[0:1, 0, :], rowsf[0:1, 0, :], rowsf[0:1, 3, :])
                        v.tensor_scalar_mul(rowsb[0:1, 1, :], rowsf[0:1, 0, :], -1.0)
                        v.tensor_scalar_mul(rowsb[0:1, 0, :], rowsf[0:1, 3, :], 1.0).then_inc(S["v"].h, 1)
                        S["v"].n += 1
                        v.wait_ge(S["t"].h, t_bc[i])
                        v.tensor_copy(bcb[:, i % 2, 0, :], PS2[:, 0, 0:TTW])
                        v.tensor_copy(bcb[:, i % 2, 1, :], PS2[:, 1, 0:TTW]).then_inc(S["v"].h, 1)
                        S["v"].n += 1
                        if i >= 1:
                            b2, tt2 = tiles[i - 1]
                            v.wait_ge(S["t"].h, t_kv[i - 1])
                            v.tensor_copy(Ksb[:, b2, tt2 * TTW:(tt2 + 1) * TTW], PS1[:, 0, 0:TTW])
                            v.tensor_copy(Vt[:, b2, 2 * tt2, 0:128], PS1[:, 1, 0:128])
                            v.tensor_copy(Vt[:, b2, 2 * tt2 + 1, 0:128],
                                          PS1[:, 1, 128:256]).then_inc(S["v"].h, 1)
                            S["v"].n += 1
                        xv = xbb[:, i % 2]
                        v.tensor_mul(xv, xv, bcb[:, i % 2, 0:1, :].broadcast_to([128, DC, TTW]))
                        if tt * TTW < QW:
                            v.tensor_add(xv, xv, bcb[:, i % 2, 1:2, :].broadcast_to([128, DC, TTW]))
                            v.tensor_copy(
                                xnown[:, :, b * QW + tt * TTW: b * QW + (tt + 1) * TTW],
                                xv).then_inc(S["v"].h, 1)
                        else:
                            v.tensor_add(xv, xv,
                                         bcb[:, i % 2, 1:2, :].broadcast_to([128, DC, TTW])).then_inc(S["v"].h, 1)
                        S["v"].n += 1
                        if i >= 1:
                            b2, tt2 = tiles[i - 1]
                            v.wait_ge(S["t"].h, t_kn[i - 1])
                            v.reduce_max(knrow[0:1, tt2:tt2 + 1], PS1[0:1, 2, 0:TTW],
                                         axis=AXX).then_inc(S["v"].h, 1)
                            S["v"].n += 1
                            if tt2 == NTT - 1:
                                v.memset(vdum[:], 0.0)
                                v.reduce_max(kmx[0:1, b2:b2 + 1], knrow[0:1, :],
                                             axis=AXX)
                                v.memset(vdum[:], 0.0).then_inc(S["v"].h, 1)
                                S["v"].n += 1
                    b2, tt2 = tiles[NTL - 1]
                    v.wait_ge(S["t"].h, t_kv[NTL - 1])
                    v.tensor_copy(Ksb[:, b2, tt2 * TTW:(tt2 + 1) * TTW], PS1[:, 0, 0:TTW])
                    v.tensor_copy(Vt[:, b2, 2 * tt2, 0:128], PS1[:, 1, 0:128])
                    v.tensor_copy(Vt[:, b2, 2 * tt2 + 1, 0:128],
                                  PS1[:, 1, 128:256]).then_inc(S["v"].h, 1)
                    S["v"].n += 1
                    v.wait_ge(S["t"].h, t_kn[NTL - 1])
                    v.reduce_max(knrow[0:1, tt2:tt2 + 1], PS1[0:1, 2, 0:TTW],
                                 axis=AXX).then_inc(S["v"].h, 1)
                    S["v"].n += 1
                    v.memset(vdum[:], 0.0)
                    v.reduce_max(kmx[0:1, b2:b2 + 1], knrow[0:1, :], axis=AXX)
                    v.memset(vdum[:], 0.0).then_inc(S["v"].h, 1)
                    S["v"].n += 1

            assert S["v"].n == vc and S["s"].n == sc_ and S["t"].n == tc, \
                (S["v"].n, vc, S["s"].n, sc_, S["t"].n, tc)

        for b in range(B):
            with ExitStack() as pb:
                Eb = pb.enter_context
                QT = Eb(nc.sbuf_tensor(f"QT{b}", [128, NH, QW], BF16))
                attnT = Eb(nc.sbuf_tensor(f"attnT{b}", [128, NH, QW], BF16))
                wbf = Eb(nc.sbuf_tensor(f"wbf{b}", [128, 3, DC, 128], BF16))
                qsqb = Eb(nc.sbuf_tensor(f"qsqb{b}", [128, 2, QW], BF16))
                qnrow = Eb(nc.sbuf_tensor(f"qnrow{b}", [1, NH], F32))
                PT = Eb(nc.sbuf_tensor(f"PT{b}", [128, 4, QW], BF16))
                attok = Eb(nc.sbuf_tensor(f"attok{b}", [128, 4, 128], BF16))
                recb = Eb(nc.sbuf_tensor(f"recb{b}", [128, 4], F32))
                xres = Eb(nc.sbuf_tensor(f"xres{b}", [128, 2, QW], F32))
                acbuf = Eb(nc.sbuf_tensor(f"acbuf{b}", [128, 2, QW], F32))

                ldwb = S["ldw"].n
                vq = S["v"].n
                sq_ = S["s"].n
                tq = S["t"].n
                v_qn = {}
                s_qev, s_qsq = {}, {}
                t_q, t_qn = {}, {}
                for oc in range(DC):
                    tq += 1; t_q[oc] = tq
                    sq_ += 1; s_qev[oc] = sq_
                    sq_ += 1; s_qsq[oc] = sq_
                    if oc >= 1:
                        tq += 1; t_qn[oc - 1] = tq
                        vq += 1; v_qn[oc - 1] = vq
                tq += 1; t_qn[DC - 1] = tq
                vq += 1; v_qn[DC - 1] = vq
                vq += 1; v_mm = vq
                sq_ += 1; s_mh = sq_
                tq += 1; t_nm = tq
                vq += 1; v_bias = vq

                with nc.Block() as blk:
                    @blk.gpsimd
                    def _(g):
                        for oc in range(DC):
                            if oc >= 3:
                                g.wait_ge(S["t"].h, t_q[oc - 3])
                            g.dma_start(out=wbf[:, oc % 3], in_=wq_d[oc]).then_inc(S["ldw"].h, 16)
                            S["ldw"].n += 16

                    @blk.vector
                    def _(v):
                        for oc in range(DC):
                            v.wait_ge(S["t"].h, t_qn[oc])
                            v.reduce_max(qnrow[0:1, oc:oc + 1], PS2[0:1, 1, 0:QW],
                                         axis=AXX).then_inc(S["v"].h, 1)
                            S["v"].n += 1
                        v.memset(vdum[:], 0.0)
                        v.memset(vdum[:], 0.0)
                        v.reduce_max(qmx[0:1, b:b + 1], qnrow[0:1, :],
                                     axis=AXX).then_inc(S["v"].h, 1)
                        S["v"].n += 1
                        v.wait_ge(S["t"].h, t_nm)
                        v.tensor_copy(negm128[:, 0:1], PS2[:, 0, 0:1])
                        v.memset(vdum[:], 0.0).then_inc(S["v"].h, 1)
                        S["v"].n += 1

                    @blk.scalar
                    def _(s):
                        for oc in range(DC):
                            s.wait_ge(S["t"].h, t_q[oc])
                            s.copy(QT[:, oc, :], PS1[:, oc % 3, 0:QW]).then_inc(S["s"].h, 1)
                            S["s"].n += 1
                            if oc >= 2:
                                s.wait_ge(S["t"].h, t_qn[oc - 2])
                            s.activation(qsqb[:, oc % 2, :], QT[:, oc, :],
                                         AF.Square).then_inc(S["s"].h, 1)
                            S["s"].n += 1
                        s.wait_ge(S["v"].h, v_mm)
                        s.mul(smallf[0:1, 0:1], qmx[0:1, b:b + 1], kmx[0:1, b:b + 1])
                        s.activation(smallf[0:1, 1:2], smallf[0:1, 0:1], AF.Sqrt)
                        s.activation(smallb[0:1, 0:1], smallf[0:1, 1:2], AF.Copy,
                                     bias=-2.0, scale=-ALPHA).then_inc(S["s"].h, 1)
                        S["s"].n += 1

                    @blk.tensor
                    def _(t):
                        for oc in range(DC):
                            t.wait_ge(S["ldw"].h, ldwb + 16 * (oc + 1))
                            if oc >= 3:
                                t.wait_ge(S["s"].h, s_qev[oc - 3])
                            for c in range(DC):
                                m = t.matmul(PS1[:, oc % 3, 0:QW], wbf[:, oc % 3, c, :],
                                             xnown[:, c, b * QW:(b + 1) * QW],
                                             start=(c == 0), stop=(c == DC - 1))
                            m.then_inc(S["t"].h, 1); S["t"].n += 1
                            if oc >= 1:
                                t.wait_ge(S["s"].h, s_qsq[oc - 1])
                                if oc >= 2:
                                    t.wait_ge(S["v"].h, v_qn[oc - 2])
                                t.matmul(PS2[0:1, 1, 0:QW], ones128[:], qsqb[:, (oc - 1) % 2, :],
                                         start=True, stop=True).then_inc(S["t"].h, 1)
                                S["t"].n += 1
                        t.wait_ge(S["s"].h, s_qsq[DC - 1])
                        t.wait_ge(S["v"].h, v_qn[DC - 2])
                        t.matmul(PS2[0:1, 1, 0:QW], ones128[:], qsqb[:, (DC - 1) % 2, :],
                                 start=True, stop=True).then_inc(S["t"].h, 1)
                        S["t"].n += 1
                        t.wait_ge(S["s"].h, s_mh)
                        t.matmul(PS2[:, 0, 0:1], ones1r[:], smallb[0:1, 0:1],
                                 start=True, stop=True).then_inc(S["t"].h, 1)
                        S["t"].n += 1

                assert S["v"].n == vq and S["s"].n == sq_ and S["t"].n == tq

                NG = NKC // 2
                t_sc, t_pv, t_tr = {}, {}, {}
                s_exp = {}
                v_mk, v_rc, v_ev, v_te = {}, {}, {}, {}
                tc3, sc3, vc3 = S["t"].n, S["s"].n, S["v"].n
                for h in range(NH):
                    for kc in range(NKC):
                        tc3 += 1; t_sc[(h, kc)] = tc3
                        if kc >= 1:
                            tc3 += 1; t_pv[(h, kc - 1)] = tc3
                        if kc == 8 and h >= 1:
                            for qs in range(4):
                                tc3 += 1; t_tr[(h - 1, qs)] = tc3
                    tc3 += 1; t_pv[(h, NKC - 1)] = tc3
                    for kc in range(NKC):
                        sc3 += 1; s_exp[(h, kc)] = sc3
                    if h >= 1:
                        vc3 += 1; v_rc[h - 1] = vc3
                        for qs in range(4):
                            vc3 += 1; v_ev[(h - 1, qs)] = vc3
                    for j in range(4):
                        vc3 += 1; v_mk[(h, j)] = vc3
                    if h >= 1:
                        for qs in range(4):
                            vc3 += 1; v_te[(h - 1, qs)] = vc3
                h = NH - 1
                vc3 += 1; v_rc[h] = vc3
                for qs in range(4):
                    vc3 += 1; v_ev[(h, qs)] = vc3
                for qs in range(4):
                    tc3 += 1; t_tr[(h, qs)] = tc3
                for qs in range(4):
                    vc3 += 1; v_te[(h, qs)] = vc3

                def pvap(h, qs):
                    off = (qs % 2) * 256
                    return PS2[:, qs // 2, off:off + 129]

                with nc.Block() as blk:
                    def ptsl(kc):
                        return PT[:, kc % 4, :]

                    @blk.tensor
                    def _(t):
                        for h in range(NH):
                            for kc in range(NKC):
                                if kc >= 4:
                                    t.wait_ge(S["s"].h, s_exp[(h, kc - 4)])
                                elif h >= 1:
                                    t.wait_ge(S["s"].h, s_exp[(h - 1, NKC - 4 + kc)])
                                t.matmul(PS1[:, kc % 4, 0:QW],
                                         Ksb[:, b, kc * 128:(kc + 1) * 128],
                                         QT[:, h, :], start=True,
                                         stop=True).then_inc(S["t"].h, 1)
                                S["t"].n += 1
                                if kc >= 1:
                                    k2 = kc - 1
                                    t.wait_ge(S["s"].h, s_exp[(h, k2)])
                                    if k2 == 0 and h >= 1:
                                        t.wait_ge(S["v"].h, v_ev[(h - 1, 3)])
                                    for qs in range(4):
                                        m = t.matmul(pvap(h, qs),
                                                     ptsl(k2)[:, qs * 128:(qs + 1) * 128],
                                                     Vt[:, b, k2, :],
                                                     start=(k2 == 0 and qs % 2 == 0),
                                                     stop=(k2 == NKC - 1),
                                                     skip_group_check=True)
                                    m.then_inc(S["t"].h, 1); S["t"].n += 1
                                if kc == 8 and h >= 1:
                                    for qs in range(4):
                                        t.wait_ge(S["v"].h, v_ev[(h - 1, qs)])
                                        prev = (h - 1, qs - 2) if qs >= 2 else \
                                               ((h - 2, qs + 2) if h >= 2 else None)
                                        if prev is not None:
                                            t.wait_ge(S["v"].h, v_te[prev])
                                        t.transpose(PTRs[qs % 2][:], attok[:, qs, :],
                                                    identsb[:]).then_inc(S["t"].h, 1)
                                        S["t"].n += 1
                            k2 = NKC - 1
                            t.wait_ge(S["s"].h, s_exp[(h, k2)])
                            for qs in range(4):
                                m = t.matmul(pvap(h, qs), ptsl(k2)[:, qs * 128:(qs + 1) * 128],
                                             Vt[:, b, k2, :], start=False, stop=True,
                                             skip_group_check=True)
                            m.then_inc(S["t"].h, 1); S["t"].n += 1
                        h = NH - 1
                        for qs in range(4):
                            t.wait_ge(S["v"].h, v_ev[(h, qs)])
                            prev = (h, qs - 2) if qs >= 2 else (h - 1, qs + 2)
                            t.wait_ge(S["v"].h, v_te[prev])
                            t.transpose(PTRs[qs % 2][:], attok[:, qs, :],
                                        identsb[:]).then_inc(S["t"].h, 1)
                            S["t"].n += 1

                    @blk.scalar
                    def _(s):
                        for h in range(NH):
                            for kc in range(NKC):
                                if kc < 4:
                                    s.wait_ge(S["v"].h, v_mk[(h, kc)])
                                else:
                                    s.wait_ge(S["t"].h, t_sc[(h, kc)])
                                if kc >= 4:
                                    s.wait_ge(S["t"].h, t_pv[(h, kc - 4)])
                                elif h >= 1:
                                    s.wait_ge(S["t"].h, t_pv[(h - 1, NKC - 4 + kc)])
                                s.activation(PT[:, kc % 4, :], PS1[:, kc % 4, 0:QW],
                                             AF.Exp, bias=negm128[:, 0:1],
                                             scale=ALPHA).then_inc(S["s"].h, 1)
                                S["s"].n += 1

                    @blk.vector
                    def _(v):
                        def ev_group(h2):
                            v.wait_ge(S["t"].h, t_pv[(h2, NKC - 1)])
                            if h2 >= 1:
                                v.wait_ge(S["v"].h, v_ev[(h2 - 1, 3)])
                            for qs in range(4):
                                m = v.reciprocal(recb[:, qs:qs + 1],
                                                 pvap(h2, qs)[:, 128:129])
                            m.then_inc(S["v"].h, 1); S["v"].n += 1
                            for qs in range(4):
                                if h2 >= 1:
                                    v.wait_ge(S["t"].h, t_tr[(h2 - 1, qs)])
                                v.tensor_scalar(attok[:, qs, :], pvap(h2, qs)[:, 0:128],
                                                recb[:, qs:qs + 1], None,
                                                OP.mult).then_inc(S["v"].h, 1)
                                S["v"].n += 1

                        for h in range(NH):
                            if h >= 1:
                                ev_group(h - 1)
                            for j in range(4):
                                v.wait_ge(S["t"].h, t_sc[(h, j)])
                                v.tensor_add(PS1[:, j, 0:QW], PS1[:, j, 0:QW],
                                             masksb[:, j, :]).then_inc(S["v"].h, 1)
                                S["v"].n += 1
                            if h >= 1:
                                for qs in range(4):
                                    v.wait_ge(S["t"].h, t_tr[(h - 1, qs)])
                                    v.tensor_copy(attnT[:, h - 1, qs * 128:(qs + 1) * 128],
                                                  PTRs[qs % 2][:]).then_inc(S["v"].h, 1)
                                    S["v"].n += 1
                        h = NH - 1
                        ev_group(h)
                        for qs in range(4):
                            v.wait_ge(S["t"].h, t_tr[(h, qs)])
                            v.tensor_copy(attnT[:, h, qs * 128:(qs + 1) * 128],
                                          PTRs[qs % 2][:]).then_inc(S["v"].h, 1)
                            S["v"].n += 1

                assert S["v"].n == vc3 and S["s"].n == sc3 and S["t"].n == tc3

                if debug:
                    with nc.Block() as blk:
                        @blk.sync
                        def _(sync):
                            sync.dma_start(out=dbg["QT"][b], in_=QT[:]).then_inc(S["sto"].h, 16)
                            sync.dma_start(out=dbg["attnT"][b], in_=attnT[:]).then_inc(S["sto"].h, 16)
                            sync.dma_start(out=dbg["negm"][b, :, None], in_=negm128[:, 0:1]).then_inc(S["sto"].h, 16)
                            sync.dma_start(out=dbg["small"][b:b + 1, :], in_=smallf[:]).then_inc(S["sto"].h, 16)
                            sync.dma_start(out=dbg["smallbb"][b:b + 1, :], in_=smallb[:]).then_inc(S["sto"].h, 16)
                            sync.dma_start(out=dbg["kq"][b:b + 1, 0:B], in_=kmx[:]).then_inc(S["sto"].h, 16)
                            sync.dma_start(out=dbg["kq"][b:b + 1, B:2 * B], in_=qmx[:]).then_inc(S["sto"].h, 16)
                            sync.dma_start(out=dbg["qnrow"][b:b + 1, :], in_=qnrow[:]).then_inc(S["sto"].h, 16)
                            S["sto"].n += 16 * 8
                            sync.wait_ge(S["sto"].h, S["sto"].n)

                ldwb4 = S["ldw"].n
                ldxb4 = S["ldx"].n
                sacb = S["sac"].n
                v4, t4 = S["v"].n, S["t"].n
                v_ace = {}
                t_wo = {}
                for oc in range(DC):
                    t4 += 1; t_wo[oc] = t4
                    v4 += 1; v_ace[oc] = v4

                with nc.Block() as blk:
                    @blk.gpsimd
                    def _(g):
                        for oc in range(DC):
                            if oc >= 3:
                                g.wait_ge(S["t"].h, t_wo[oc - 3])
                            g.dma_start(out=wbf[:, oc % 3], in_=wo_d[oc]).then_inc(S["ldw"].h, 16)
                            S["ldw"].n += 16

                    @blk.sync
                    def _(sync):
                        for oc in range(DC):
                            if oc >= 2:
                                sync.wait_ge(S["v"].h, v_ace[oc - 2])
                            sync.dma_start(out=xres[:, oc % 2],
                                           in_=xown_d[b, oc]).then_inc(S["ldx"].h, 16)
                            S["ldx"].n += 16

                    @blk.vector
                    def _(v):
                        for oc in range(DC):
                            v.wait_ge(S["t"].h, t_wo[oc])
                            v.wait_ge(S["ldx"].h, ldxb4 + 16 * (oc + 1))
                            if oc >= 2:
                                v.wait_ge(S["sac"].h, sacb + 16 * (oc - 1))
                            v.tensor_add(acbuf[:, oc % 2, :], PS1[:, oc % 2, 0:QW],
                                         xres[:, oc % 2, :]).then_inc(S["v"].h, 1)
                            S["v"].n += 1

                    @blk.scalar
                    def _(s):
                        for oc in range(DC):
                            s.wait_ge(S["v"].h, v_ace[oc])
                            s.dma_start(out=attnC[b, oc * 128:(oc + 1) * 128, :],
                                        in_=acbuf[:, oc % 2, :]).then_inc(S["sac"].h, 16)
                            S["sac"].n += 16

                    @blk.tensor
                    def _(t):
                        for oc in range(DC):
                            t.wait_ge(S["ldw"].h, ldwb4 + 16 * (oc + 1))
                            if oc >= 2:
                                t.wait_ge(S["v"].h, v_ace[oc - 2])
                            for c in range(DC):
                                m = t.matmul(PS1[:, oc % 2, 0:QW], wbf[:, oc % 3, c, :],
                                             attnT[:, c, :], start=(c == 0), stop=(c == DC - 1))
                            m.then_inc(S["t"].h, 1); S["t"].n += 1

                assert S["v"].n == v4 and S["t"].n == t4

        sac_total = S["sac"].n
        if not debug:
            attn_scope.close()

        if debug:
            with nc.Block() as blk:
                @blk.sync
                def _(sync):
                    sync.dma_start(out=dbg["xnown"][:, :, :], in_=xnown[:]).then_inc(S["sto"].h, 16)
                    sync.dma_start(out=dbg["K"][:, :, :], in_=Ksb[:]).then_inc(S["sto"].h, 16)
                    sync.dma_start(out=dbg["Vt"][:, :, :, :], in_=Vt[:]).then_inc(S["sto"].h, 16)
                    S["sto"].n += 48
                    sync.wait_ge(S["sto"].h, S["sto"].n)

                @blk.gpsimd
                def _(g):
                    g.wait_ge(S["sac"].h, sac_total)
                    for bb in range(B):
                        g.dma_start(out=dbg["attnC"][bb], in_=attnC[bb]).then_inc(S["sto"].h, 16)
                        S["sto"].n += 16
                    g.wait_ge(S["sto"].h, S["sto"].n)
        if debug:
            attn_scope.close()

        with ExitStack() as p5:
            E5 = p5.enter_context
            hbuf = E5(nc.sbuf_tensor("hbuf", [128, 29, B * QW], BF16))
            xAg = E5(nc.sbuf_tensor("xAg", [32, 2, 512], BF16))
            xAu = E5(nc.sbuf_tensor("xAu", [32, 2, 512], BF16))
            dAbf = E5(nc.sbuf_tensor("dAbf", [128, ICH, 32], BF16))
            hdacc = E5(nc.sbuf_tensor("hdacc", [32, B * QW], F32))
            hdbf = E5(nc.sbuf_tensor("hdbf", [32, B * QW], BF16))

            pro_scope = ExitStack()
            Abf = pro_scope.enter_context(nc.sbuf_tensor("Abf", [128, DC, 32], BF16))
            with nc.Block() as blk:
                @blk.gpsimd
                def _(g):
                    g.dma_start(out=Abf[:], in_=gA_d[:, :, :]).then_inc(S["ldc"].h, 16)
                    g.dma_start(out=dAbf[:], in_=dA_d[:, :, :]).then_inc(S["ldc"].h, 16)
                    S["ldc"].n += 32

                @blk.vector
                def _(v):
                    v.memset(hdacc[:], 0.0)
                    v.wait_ge(S["ldc"].h, S["ldc"].n)

            for Ad, xA in ((gA_d, xAg), (uA_d, xAu)):
                tpro = S["t"].n
                with nc.Block() as blk:
                    @blk.tensor
                    def _(t):
                        for w in range(2):
                            if w == 1:
                                t.wait_ge(S["v"].h, S["v"].n + 1)
                            for c in range(DC):
                                m = t.matmul(PS1[0:32, 0, 0:QW], Abf[:, c, :],
                                             xnown[:, c, w * QW:(w + 1) * QW],
                                             start=(c == 0), stop=(c == DC - 1))
                            m.then_inc(S["t"].h, 1); S["t"].n += 1

                    @blk.vector
                    def _(v):
                        for w in range(2):
                            v.wait_ge(S["t"].h, tpro + w + 1)
                            v.tensor_scalar_mul(xA[:, w, :], PS1[0:32, 0, 0:QW],
                                                LR_SCALE).then_inc(S["v"].h, 1)
                            S["v"].n += 1

                if xA is xAg:
                    with nc.Block() as blk:
                        @blk.gpsimd
                        def _(g):
                            g.dma_start(out=Abf[:], in_=uA_d[:, :, :]).then_inc(S["ldc"].h, 16)
                            S["ldc"].n += 16

                        @blk.vector
                        def _(v):
                            v.wait_ge(S["ldc"].h, S["ldc"].n)

            pro_scope.close()

            for p in range(NPASS):
                oc0, oc1 = OCSPL[p], OCSPL[p + 1]
                nic = oc1 - oc0

                gu_scope = ExitStack()
                gbf = gu_scope.enter_context(nc.sbuf_tensor(f"gbf{p}", [128, 2, 2, DC, 128], BF16))
                brow = gu_scope.enter_context(nc.sbuf_tensor(f"brow{p}", [32, 2, 2, 128], BF16))
                t1g = gu_scope.enter_context(nc.sbuf_tensor(f"t1g{p}", [128, 2, 512], F32))
                sgb = gu_scope.enter_context(nc.sbuf_tensor(f"sgb{p}", [128, 2, 512], BF16))
                uub = gu_scope.enter_context(nc.sbuf_tensor(f"uub{p}", [128, 2, 512], BF16))
                ldwb5 = S["ldw"].n
                v5, s5, t5 = S["v"].n, S["s"].n, S["t"].n
                v_cv, v_evg, v_evu, v_hm = {}, {}, {}, {}
                s_si = {}
                t_gu, t_lr = {}, {}
                for j in range(nic):
                    for ty in range(2):
                        seq = 2 * j + ty
                        v5 += 1; v_cv[seq] = v5
                        t5 += 1; t_gu[(j, ty)] = t5
                        t5 += 1; t_lr[(j, ty)] = t5
                    v5 += 1; v_evg[j] = v5
                    s5 += 1; s_si[j] = s5
                    v5 += 1; v_evu[j] = v5
                    v5 += 1; v_hm[j] = v5

                with nc.Block() as blk:
                    @blk.gpsimd
                    def _(g):
                        for j in range(nic):
                            oc = oc0 + j
                            for ty, wd, bd in ((0, g_d, gB_d), (1, u_d, uB_d)):
                                seq = 2 * j + ty
                                if j >= 2:
                                    g.wait_ge(S["t"].h, t_lr[(j - 2, ty)])
                                g.dma_start(out=gbf[:, ty, j % 2], in_=wd[oc]).then_inc(S["ldw"].h, 16)
                                g.dma_start(out=brow[:, ty, j % 2, :],
                                            in_=bd[:, oc * 128:(oc + 1) * 128]).then_inc(S["ldw"].h, 16)
                                S["ldw"].n += 32

                    @blk.vector
                    def _(v):
                        for j in range(nic):
                            for ty in range(2):
                                seq = 2 * j + ty
                                v.wait_ge(S["ldw"].h, ldwb5 + 32 * (seq + 1))
                                ga = gbf[:, ty, j % 2].rearrange("p c o -> p (c o)")
                                v.tensor_scalar_add(ga, ga, Q_OFF).then_inc(S["v"].h, 1)
                                S["v"].n += 1
                            v.wait_ge(S["t"].h, t_lr[(j, 0)])
                            if j >= 1:
                                v.wait_ge(S["s"].h, s_si[j - 1])
                            sb = sgates[:, oc0 + j:oc0 + j + 1].broadcast_to([128, QW])
                            for w in range(2):
                                v.tensor_mul(t1g[:, w, :], PS1[:, w, :QW], sb)
                            v.tensor_add(t1g[:, 0, :], t1g[:, 0, :], PS2[:, 0, :QW])
                            v.tensor_add(t1g[:, 1, :], t1g[:, 1, :],
                                         PS2[:, 1, :QW]).then_inc(S["v"].h, 1)
                            S["v"].n += 1
                            v.wait_ge(S["t"].h, t_lr[(j, 1)])
                            ub = sups[:, oc0 + j:oc0 + j + 1].broadcast_to([128, QW])
                            v.tensor_mul(uub[:, 0, :], PS1[:, 2, :QW], ub)
                            v.tensor_mul(uub[:, 1, :], PS1[:, 3, 0:QW], ub)
                            v.tensor_add(uub[:, 0, :], uub[:, 0, :], PS2[:, 0, :QW])
                            v.tensor_add(uub[:, 1, :], uub[:, 1, :],
                                         PS2[:, 1, :QW]).then_inc(S["v"].h, 1)
                            S["v"].n += 1
                            v.wait_ge(S["s"].h, s_si[j])
                            v.tensor_mul(hbuf[:, j, :].rearrange("p (w q) -> p w q", w=2),
                                         sgb[:, :, :], uub[:, :, :]).then_inc(S["v"].h, 1)
                            S["v"].n += 1

                    @blk.scalar
                    def _(s):
                        for j in range(nic):
                            s.wait_ge(S["v"].h, v_evg[j])
                            s.activation(sgb[:].rearrange("p w q -> p (w q)"),
                                         t1g[:].rearrange("p w q -> p (w q)"),
                                         AF.Silu).then_inc(S["s"].h, 1)
                            S["s"].n += 1

                    @blk.tensor
                    def _(t):
                        for j in range(nic):
                            for ty in range(2):
                                seq = 2 * j + ty
                                t.wait_ge(S["v"].h, v_cv[seq])
                                if j >= 1:
                                    t.wait_ge(S["v"].h, (v_evg if ty == 0 else v_evu)[j - 1])
                                for c in range(DC):
                                    for w in range(2):
                                        dstp = (PS1[:, w, 0:QW] if ty == 0 else
                                                (PS1[:, 2, 0:QW] if w == 0 else PS1[:, 3, 0:QW]))
                                        m = t.matmul(dstp, gbf[:, ty, j % 2, c, :],
                                                     xnown[:, c, w * QW:(w + 1) * QW],
                                                     start=(c == 0), stop=(c == DC - 1))
                                m.then_inc(S["t"].h, 1); S["t"].n += 1
                                xA = xAg if ty == 0 else xAu
                                if ty == 0 and j >= 1:
                                    t.wait_ge(S["v"].h, v_evu[j - 1])
                                if ty == 1:
                                    t.wait_ge(S["v"].h, v_evg[j])
                                for w in range(2):
                                    m = t.matmul(PS2[:, w, 0:QW], brow[:, ty, j % 2, :], xA[:, w, :],
                                                 start=True, stop=True)
                                m.then_inc(S["t"].h, 1); S["t"].n += 1

                assert S["v"].n == v5 and S["s"].n == s5 and S["t"].n == t5

                gu_scope.close()

                if p == NPASS - 1:
                    dB_scope = ExitStack()
                    dBbf = dB_scope.enter_context(nc.sbuf_tensor("dBbf", [32, D], BF16))
                thd = S["t"].n
                with nc.Block() as blk:
                    if p == NPASS - 1:
                        @blk.gpsimd
                        def _(g):
                            g.dma_start(out=dBbf[:], in_=dB_d[:, :]).then_inc(S["ldc"].h, 16)
                            S["ldc"].n += 16

                    @blk.tensor
                    def _(t):
                        for w in range(2):
                            if w == 1:
                                t.wait_ge(S["v"].h, S["v"].n + 1)
                            for j in range(nic):
                                m = t.matmul(PS2[0:32, 0, 0:QW], dAbf[:, oc0 + j, :],
                                             hbuf[:, j, w * QW:(w + 1) * QW],
                                             start=(j == 0), stop=(j == nic - 1))
                            m.then_inc(S["t"].h, 1); S["t"].n += 1

                    @blk.vector
                    def _(v):
                        for w in range(2):
                            v.wait_ge(S["t"].h, thd + w + 1)
                            v.tensor_add(hdacc[:, w * QW:(w + 1) * QW],
                                         hdacc[:, w * QW:(w + 1) * QW],
                                         PS2[0:32, 0, 0:QW]).then_inc(S["v"].h, 1)
                            S["v"].n += 1
                        if p == NPASS - 1:
                            v.memset(vdum[:], 0.0)
                            v.tensor_scalar_mul(hdbf[:], hdacc[:], LR_SCALE)

                dn_scope = ExitStack()
                dbfb = dn_scope.enter_context(nc.sbuf_tensor(f"dbfb{p}", [128, 3, 29, 128], BF16))
                pstage = dn_scope.enter_context(nc.sbuf_tensor(f"pstage{p}", [128, 2, B * QW], F32))
                ac0 = dn_scope.enter_context(nc.sbuf_tensor(f"ac0_{p}", [128, 2, QW], F32))
                ac1 = dn_scope.enter_context(nc.sbuf_tensor(f"ac1_{p}", [128, 2, QW], F32))
                t1d = dn_scope.enter_context(nc.sbuf_tensor(f"t1d{p}", [128, 2, 2, 512], F32))
                ldwbD = S["ldw"].n
                pldb = S["pld"].n
                sppb = S["spp"].n
                stob = S["sto"].n
                last = (p == NPASS - 1)
                per_pld = (16 if p > 0 else 0) + (32 if last else 0)
                vD, tD = S["v"].n, S["t"].n
                v_dcv, v_dev = {}, {}
                t_dn, t_dlr = {}, {}
                for Hc in range(DC):
                    vD += 1; v_dcv[Hc] = vD
                    tD += 1; t_dn[(Hc, 0)] = tD
                    tD += 1; t_dn[(Hc, 1)] = tD
                    if last:
                        tD += 1; t_dlr[Hc] = tD
                    if Hc >= 1:
                        vD += 1; v_dev[Hc - 1] = vD
                vD += 1; v_dev[DC - 1] = vD

                with nc.Block() as blk:
                    @blk.gpsimd
                    def _(g):
                        for Hc in range(DC):
                            if Hc >= 3:
                                g.wait_ge(S["t"].h, t_dn[(Hc - 3, 1)])
                            g.dma_start(out=dbfb[:, Hc % 3, 0:nic, :],
                                        in_=dn_d[Hc, :, oc0:oc1, :]).then_inc(S["ldw"].h, 16)
                            S["ldw"].n += 16

                    @blk.sync
                    def _(sync):
                        for Hc in range(DC):
                            if per_pld:
                                if Hc >= 2:
                                    sync.wait_ge(S["v"].h, v_dev[Hc - 2])
                                if last and Hc == 0:
                                    sync.wait_ge(S["sac"].h, sac_total)
                                if p > 0 and Hc == 0:
                                    sync.wait_ge(S["spp"].h, sppb)
                                if p > 0:
                                    sync.dma_start(out=pstage[:, Hc % 2, :],
                                                   in_=partial[Hc]).then_inc(S["pld"].h, 16)
                                    S["pld"].n += 16
                                if last:
                                    sync.dma_start(out=ac0[:, Hc % 2, :],
                                                   in_=attnC[0, Hc * 128:(Hc + 1) * 128, :]).then_inc(S["pld"].h, 16)
                                    sync.dma_start(out=ac1[:, Hc % 2, :],
                                                   in_=attnC[1, Hc * 128:(Hc + 1) * 128, :]).then_inc(S["pld"].h, 16)
                                    S["pld"].n += 32

                    @blk.vector
                    def _(v):
                        def devac(Hc):
                            if per_pld:
                                v.wait_ge(S["pld"].h, pldb + per_pld * (Hc + 1))
                            if Hc >= 2:
                                if last:
                                    v.wait_ge(S["sto"].h, stob + 32 * (Hc - 1))
                                else:
                                    v.wait_ge(S["spp"].h, sppb + 16 * (Hc - 1))
                            v.wait_ge(S["t"].h, t_dlr[Hc] if last else t_dn[(Hc, 1)])
                            sdb = sdowns[:, Hc:Hc + 1].broadcast_to([128, QW])
                            for w in range(2):
                                dst = t1d[:, Hc % 2, w, :]
                                if Hc % 2 == 0:
                                    srcp = PS1[:, w, 0:QW]
                                else:
                                    srcp = PS1[:, 2, 0:QW] if w == 0 else PS1[:, 3, 0:QW]
                                if p == 0:
                                    v.tensor_mul(dst, srcp, sdb)
                                else:
                                    v.scalar_tensor_tensor(dst, srcp,
                                                           sdowns[:, Hc:Hc + 1],
                                                           pstage[:, Hc % 2, w * QW:(w + 1) * QW],
                                                           OP.mult, OP.add)
                            if last:
                                for w in range(2):
                                    v.tensor_add(t1d[:, Hc % 2, w, :], t1d[:, Hc % 2, w, :],
                                                 PS2[:, w, 0:QW])
                                v.tensor_add(t1d[:, Hc % 2, 0, :], t1d[:, Hc % 2, 0, :],
                                             ac0[:, Hc % 2, :])
                                v.tensor_add(t1d[:, Hc % 2, 1, :], t1d[:, Hc % 2, 1, :],
                                             ac1[:, Hc % 2, :]).then_inc(S["v"].h, 1)
                            else:
                                v.memset(vdum[:], 0.0).then_inc(S["v"].h, 1)
                            S["v"].n += 1

                        for Hc in range(DC):
                            v.wait_ge(S["ldw"].h, ldwbD + 16 * (Hc + 1))
                            ap = dbfb[:, Hc % 3, 0:nic, :].rearrange("p c o -> p (c o)")
                            v.tensor_scalar_add(ap, ap, Q_OFF).then_inc(S["v"].h, 1)
                            S["v"].n += 1
                            if Hc >= 1:
                                devac(Hc - 1)
                        devac(DC - 1)

                    @blk.scalar
                    def _(sg):
                        for Hc in range(DC):
                            sg.wait_ge(S["v"].h, v_dev[Hc])
                            if not last:
                                sg.dma_start(out=partial[Hc],
                                             in_=t1d[:, Hc % 2].rearrange("p w q -> p (w q)")).then_inc(S["spp"].h, 16)
                                S["spp"].n += 16
                            else:
                                sg.dma_start(out=youts[0][Hc * 128:(Hc + 1) * 128, :],
                                             in_=t1d[:, Hc % 2, 0, :]).then_inc(S["sto"].h, 16)
                                sg.dma_start(out=youts[1][Hc * 128:(Hc + 1) * 128, :],
                                             in_=t1d[:, Hc % 2, 1, :]).then_inc(S["sto"].h, 16)
                                S["sto"].n += 32

                    @blk.tensor
                    def _(t):
                        for Hc in range(DC):
                            t.wait_ge(S["v"].h, v_dcv[Hc])
                            if Hc >= 2:
                                t.wait_ge(S["v"].h, v_dev[Hc - 2])
                            for j in range(nic):
                                for w in range(2):
                                    if Hc % 2 == 0:
                                        dstp = PS1[:, w, 0:QW]
                                    else:
                                        dstp = PS1[:, 2, 0:QW] if w == 0 else PS1[:, 3, 0:QW]
                                    m = t.matmul(dstp, dbfb[:, Hc % 3, j, :],
                                                 hbuf[:, j, w * QW:(w + 1) * QW],
                                                 start=(j == 0), stop=(j == nic - 1))
                                    if j == nic - 1:
                                        m.then_inc(S["t"].h, 1); S["t"].n += 1
                            if last:
                                if Hc == 0:
                                    t.wait_ge(S["ldc"].h, S["ldc"].n)
                                if Hc >= 1:
                                    t.wait_ge(S["v"].h, v_dev[Hc - 1])
                                for w in range(2):
                                    m = t.matmul(PS2[:, w, 0:QW],
                                                 dBbf[:, Hc * 128:(Hc + 1) * 128],
                                                 hdbf[:, w * QW:(w + 1) * QW],
                                                 start=True, stop=True)
                                m.then_inc(S["t"].h, 1); S["t"].n += 1

                assert S["v"].n == vD and S["t"].n == tD
                dn_scope.close()
                if p == NPASS - 1:
                    dB_scope.close()

            if debug:
                with nc.Block() as blk:
                    @blk.sync
                    def _(sync):
                        sync.dma_start(out=dbg["hbuf"][:, :, :], in_=hbuf[:, 0:28, :]).then_inc(S["sto"].h, 16)
                        sync.dma_start(out=dbg["xA"][:, 0:2, :], in_=xAg[:]).then_inc(S["sto"].h, 16)
                        sync.dma_start(out=dbg["xA"][:, 2:4, :], in_=xAu[:]).then_inc(S["sto"].h, 16)
                        sync.dma_start(out=dbg["hd"][:, :], in_=hdacc[:]).then_inc(S["sto"].h, 16)
                        S["sto"].n += 64
                        sync.wait_ge(S["sto"].h, S["sto"].n)

            with nc.Block() as blk:
                @blk.sync
                def _(sync):
                    sync.wait_ge(S["sto"].h, S["sto"].n)

    return nc


_CACHE = {}


def _prep_shared(inputs):
    d = {}
    wq = np.ascontiguousarray(inputs["L0_attn_wq"], dtype=np.float32)
    wo = np.ascontiguousarray(inputs["L0_attn_wo"], dtype=np.float32)
    d["wq_blk"] = np.ascontiguousarray(wq.reshape(DC, 128, DC, 128).transpose(2, 1, 0, 3))
    d["wo_blk"] = np.ascontiguousarray(wo.reshape(DC, 128, DC, 128).transpose(2, 1, 0, 3))
    d["wk"] = np.ascontiguousarray(inputs["L0_attn_wk"], dtype=np.float32)
    d["wv"] = np.ascontiguousarray(inputs["L0_attn_wv"], dtype=np.float32)
    gq = np.ascontiguousarray(inputs["L0_ffn_gate_q"].T)
    uq = np.ascontiguousarray(inputs["L0_ffn_up_q"].T)
    dq = np.ascontiguousarray(inputs["L0_ffn_down_q"].T)
    d["gate_blk"] = np.ascontiguousarray(gq.reshape(DC, 128, ICH, 128).transpose(2, 1, 0, 3))
    d["up_blk"] = np.ascontiguousarray(uq.reshape(DC, 128, ICH, 128).transpose(2, 1, 0, 3))
    d["down_blk"] = np.ascontiguousarray(dq.reshape(ICH, 128, DC, 128).transpose(2, 1, 0, 3))
    d["gsc"] = np.ascontiguousarray(np.asarray(inputs["L0_ffn_gate_scale"], dtype=np.float32).reshape(ICH, 128).T)
    d["usc"] = np.ascontiguousarray(np.asarray(inputs["L0_ffn_up_scale"], dtype=np.float32).reshape(ICH, 128).T)
    d["dsc"] = np.ascontiguousarray(np.asarray(inputs["L0_ffn_down_scale"], dtype=np.float32).reshape(DC, 128).T)
    for k, name in (("gB", "L0_ffn_gate_B"), ("uB", "L0_ffn_up_B"),
                    ("dB", "L0_ffn_down_B")):
        d[k] = np.ascontiguousarray(inputs[name], dtype=np.float32)
    d["gA"] = np.ascontiguousarray(np.asarray(inputs["L0_ffn_gate_A"], dtype=np.float32)
                                   .reshape(DC, 128, 32).transpose(1, 0, 2))
    d["uA"] = np.ascontiguousarray(np.asarray(inputs["L0_ffn_up_A"], dtype=np.float32)
                                   .reshape(DC, 128, 32).transpose(1, 0, 2))
    d["dA"] = np.ascontiguousarray(np.asarray(inputs["L0_ffn_down_A"], dtype=np.float32)
                                   .reshape(ICH, 128, 32).transpose(1, 0, 2))
    return d


def _prep_core(x, c):
    klen = QW * (c + 1)
    xkv = np.zeros((B, D, T), dtype=np.float32)
    for b in range(B):
        xt = x[b].T
        xkv[b, :, 0:QW] = xt[:, QW * c:QW * (c + 1)]
        if c > 0:
            xkv[b, :, QW:klen] = xt[:, 0:QW * c]
    xb = np.ascontiguousarray(
        xkv.reshape(B, DC, 128, NTT, TTW).transpose(0, 3, 2, 1, 4))
    xo = np.ascontiguousarray(
        xkv[:, :, 0:QW].reshape(B, DC, 128, QW))
    vones = np.zeros((B, NKC), dtype=np.float32)
    vones[:, 0:klen // 128] = 1.0
    return {"xkv": xb, "xown": xo, "vones": vones}


def kernel(**inputs):
    if "nc" not in _CACHE:
        _CACHE["nc"] = build_program(debug=False)
    nc = _CACHE["nc"]
    shared = _prep_shared(inputs)
    x = np.ascontiguousarray(np.asarray(inputs["L0_attn_x"], dtype=np.float32))
    in_maps = []
    for c in range(8):
        m = dict(shared)
        m.update(_prep_core(x, c))
        in_maps.append(m)
    res = run_bass_kernel_spmd(nc, in_maps, core_ids=list(range(8)))
    out = np.empty((B, T, D), dtype=np.float32)
    for c in range(8):
        r = res.results[c]
        out[0, QW * c:QW * (c + 1), :] = np.asarray(r["y0"]).T
        out[1, QW * c:QW * (c + 1), :] = np.asarray(r["y1"]).T
    return out
